# version 11
# baseline (speedup 1.0000x reference)
import numpy as np
import ml_dtypes
import concourse.bass as bass
import concourse.mybir as mybir
import concourse.bass_utils as _bu
from concourse.bass_utils import run_bass_kernel_spmd

F32 = mybir.dt.float32
BF16 = mybir.dt.bfloat16
I8 = mybir.dt.int8
AF = mybir.ActivationFunctionType
ALU = mybir.AluOpType
AX = mybir.AxisListType

NCORE = 8
TCORE = 2048
NT = 2
TT = 1024
NS = 8
NCH = 8
NU = 32
NRING = 8
NPART = 4
NICP = 11
SCALE = 0.08838834764831845
DM = 2048


def build_plan(tau):
    items = []
    for part in range(NPART):
        for icH in range(NICP):
            for sub in range(4):
                items.append(("gu", part, icH, sub))
        if part < 3:
            for ib in range(32):
                items.append(("dn", part, ib))
        else:
            for i in range(32):
                items.append(("dn", part, i))
                if i >= 1:
                    items.append(("tr", i - 1))
            items.append(("tr", 31))
    merged = []
    ai = 0
    ci = 0
    for idx, it in enumerate(items):
        while ai < NU and idx >= int(1.5 * ai):
            merged.append(("attn", ai))
            ai += 1
        if tau == 1 and idx == 97:
            merged.append(("cc",))
        merged.append(it)
    while ai < NU:
        merged.append(("attn", ai))
        ai += 1
    return merged


PLANS = [build_plan(0), build_plan(1)]

GIDX = {}
_g = 0
for _tau in range(NT):
    for _it in PLANS[_tau]:
        if _it[0] in ("gu", "dn", "tr", "corr"):
            GIDX[(_tau,) + _it] = _g
            _g += 1
NGRP = _g


def build_nc():
    nc = bass.Bass("TRN2", target_bir_lowering=False)

    x_h = nc.dram_tensor("x", [TCORE, DM], F32, kind="ExternalInput")
    wq_h = nc.dram_tensor("wq_s", [16, 128, 2048], F32, kind="ExternalInput")
    wk_h = nc.dram_tensor("wk_s", [4, 128, 2048], F32, kind="ExternalInput")
    wv_h = nc.dram_tensor("wv_s", [4, 128, 2048], F32, kind="ExternalInput")
    wgu_h = nc.dram_tensor("wgu_s", [44, 128, 4096], F32, kind="ExternalInput")
    wd_h = nc.dram_tensor("wd_s", [44, 128, 2048], F32, kind="ExternalInput")
    bqt_h = nc.dram_tensor("bqt", [128, 16], F32, kind="ExternalInput")
    bkt_h = nc.dram_tensor("bkt", [128, 4], F32, kind="ExternalInput")
    bvt_h = nc.dram_tensor("bvt", [128, 4], F32, kind="ExternalInput")
    gst_h = nc.dram_tensor("gst", [128, 44], F32, kind="ExternalInput")
    ust_h = nc.dram_tensor("ust", [128, 44], F32, kind="ExternalInput")
    dst_h = nc.dram_tensor("dst", [128, 16], F32, kind="ExternalInput")
    idn_h = nc.dram_tensor("identf", [128, 128], F32, kind="ExternalInput")
    msk_h = nc.dram_tensor("mask8", [128, 512], I8, kind="ExternalInput")
    cm_h = nc.dram_tensor("cmask", [128, 8], F32, kind="ExternalInput")
    cm1_h = nc.dram_tensor("cmask1", [128, 8], F32, kind="ExternalInput")

    out_h = nc.dram_tensor("out", [TCORE, DM], F32, kind="ExternalOutput")

    o_scr = nc.dram_tensor("o_scr", [16, 128, 2048], F32, kind="Internal")
    qh_scr = nc.dram_tensor("qh_scr", [16, 4, 128, 512], BF16, kind="Internal")
    dnp_scr = nc.dram_tensor("dnp_scr", [16, 128, 2048], F32, kind="Internal")
    dn_scr = nc.dram_tensor("dn_scr", [16, 128, 2048], F32, kind="Internal")
    cc_in = nc.dram_tensor("cc_in", [4, 128, 129], F32, kind="Internal")
    cc_out = nc.dram_tensor("cc_out", [32, 128, 129], F32, kind="Internal")

    ones_f = nc.alloc_sbuf_tensor("ones_f", [128, 128], F32)
    ident = nc.alloc_sbuf_tensor("ident", [128, 128], F32)
    mask8 = nc.alloc_sbuf_tensor("mask8s", [128, 512], I8)
    cmb = nc.alloc_sbuf_tensor("cmb", [128, 8], F32)
    cm1b = nc.alloc_sbuf_tensor("cm1b", [128, 8], F32)
    bqt = nc.alloc_sbuf_tensor("bqt_s", [128, 16], F32)
    bqs = nc.alloc_sbuf_tensor("bqs_s", [128, 16], F32)
    bkt = nc.alloc_sbuf_tensor("bkt_s", [128, 4], F32)
    nbkt = nc.alloc_sbuf_tensor("nbkt_s", [128, 4], F32)
    bvt = nc.alloc_sbuf_tensor("bvt_s", [128, 4], F32)
    gst = nc.alloc_sbuf_tensor("gst_s", [128, 44], F32)
    ust = nc.alloc_sbuf_tensor("ust_s", [128, 44], F32)
    dst = nc.alloc_sbuf_tensor("dst_s", [128, 16], F32)
    stat1 = nc.alloc_sbuf_tensor("stat1", [128, 16], F32)
    stat2 = nc.alloc_sbuf_tensor("stat2", [128, 16], F32)
    meanv = nc.alloc_sbuf_tensor("meanv", [128, 16], F32)
    em2 = nc.alloc_sbuf_tensor("em2", [128, 16], F32)
    varb = nc.alloc_sbuf_tensor("varb", [128, 16], F32)
    lnv = nc.alloc_sbuf_tensor("lnv", [128, 16], F32)
    rstd = nc.alloc_sbuf_tensor("rstd", [128, 16], F32)
    nmr = nc.alloc_sbuf_tensor("nmr", [128, 16], F32)
    acs4 = nc.alloc_sbuf_tensor("acs4", [128, 4], F32)
    bmacs = nc.alloc_sbuf_tensor("bmacs", [128, 64], F32)
    dtot4 = nc.alloc_sbuf_tensor("dtot4", [128, 4], F32)
    eab = nc.alloc_sbuf_tensor("eab", [128, 64], F32)
    s_state = nc.alloc_sbuf_tensor("s_state", [128, 4 * 128], F32)
    spack = nc.alloc_sbuf_tensor("spack", [128, 4 * 129], F32)
    rfold = nc.alloc_sbuf_tensor("rfold", [128, 4 * 129], F32)
    dpcol4 = nc.alloc_sbuf_tensor("dpcol4", [128, 4], F32)
    epsb = nc.alloc_sbuf_tensor("epsb", [128, 1], F32)
    ebring = nc.alloc_sbuf_tensor("ebring", [128, NRING * 2], F32)
    ering = nc.alloc_sbuf_tensor("ering", [128, NRING * 512], BF16)
    kring = nc.alloc_sbuf_tensor("kring", [128, NRING * 128], BF16)
    khring = nc.alloc_sbuf_tensor("khring", [128, NRING * 128], F32)
    gbuf = nc.alloc_sbuf_tensor("gbuf", [128, 512], F32)
    bbuf3 = nc.alloc_sbuf_tensor("bbuf3", [128, 3 * 512], F32)
    nb4r = nc.alloc_sbuf_tensor("nb4r", [128, NRING * 4], F32)
    kexpb = nc.alloc_sbuf_tensor("kexpb", [128, 4 * 128], F32)
    kh2eb = nc.alloc_sbuf_tensor("kh2eb", [128, 4 * 128], F32)
    ekb = nc.alloc_sbuf_tensor("ekb", [128, 2 * 512], F32)
    kpost = nc.alloc_sbuf_tensor("kpost", [128, 512], F32)
    kpost2 = nc.alloc_sbuf_tensor("kpost2", [128, 512], F32)
    rtmp4 = nc.alloc_sbuf_tensor("rtmp4", [128, 4 * 129], F32)
    qside = nc.alloc_sbuf_tensor("qside", [128, 4 * 512], BF16)
    stil = nc.alloc_sbuf_tensor("stil", [128, 2 * 512], BF16)
    khtb = nc.alloc_sbuf_tensor("khtb", [128, 2 * 128], BF16)
    sscal = nc.alloc_sbuf_tensor("sscal", [128, 2 * 128], BF16)
    scorb = nc.alloc_sbuf_tensor("scorb", [128, 4 * 128], BF16)
    corrstg = nc.alloc_sbuf_tensor("corrstg", [128, 512], F32)
    ostg = nc.alloc_sbuf_tensor("ostg", [128, 512], F32)
    sgb = nc.alloc_sbuf_tensor("sgb", [128, 2 * 512], BF16)
    dnstg = nc.alloc_sbuf_tensor("dnstg", [128, 2 * 512], F32)
    dnpld = nc.alloc_sbuf_tensor("dnpld", [128, 2 * 512], F32)
    trstg = nc.alloc_sbuf_tensor("trstg", [128, 512], F32)
    vstg = nc.alloc_sbuf_tensor("vstg", [128, 2 * 512], F32)
    wdbuf = nc.alloc_sbuf_tensor("wdbuf", [128, 2 * 1408], BF16)
    wun = nc.alloc_sbuf_tensor("wun", [128, 8192], BF16)

    AR = (nc.sbuf_base + 63) & ~63
    assert AR + 112 * 1024 <= nc.sbuf_top, (AR, nc.sbuf_top)
    xnT = nc.alloc_sbuf_tensor_at("xnT", [128, 16 * 1024], BF16, offset=AR + 0)
    qs = nc.alloc_sbuf_tensor_at("qs", [128, 16 * 1024], BF16, offset=AR + 32 * 1024)
    kf = nc.alloc_sbuf_tensor_at("kf", [128, 4 * 1024], F32, offset=AR + 64 * 1024)
    vtok = nc.alloc_sbuf_tensor_at("vtok", [128, 8 * 512], BF16, offset=AR + 80 * 1024)
    qhld = nc.alloc_sbuf_tensor_at("qhld", [128, 4 * 512], BF16, offset=AR + 84 * 1024)
    hbuf = nc.alloc_sbuf_tensor_at("hbuf", [128, NICP * 1024], BF16, offset=AR + 88 * 1024)
    xn_tok = nc.alloc_sbuf_tensor_at("xn_tok", [128, 2048], F32, offset=AR + 88 * 1024)
    xbuf = nc.alloc_sbuf_tensor_at("xbuf", [128, 2 * 2048], F32, offset=AR + 96 * 1024)
    fsum = nc.alloc_sbuf_tensor_at("fsum", [128, 4 * 2048], F32, offset=AR + 32 * 1024)
    sqscr = nc.alloc_sbuf_tensor_at("sqscr", [128, 2048], F32, offset=AR + 32 * 1024)
    cstg = nc.alloc_sbuf_tensor_at("cstg", [128, 8 * 512], F32, offset=AR + 0)
    ccst = nc.alloc_sbuf_tensor_at("ccst", [128, 32 * 129], F32, offset=AR + 64 * 1024)

    ps_mm = [nc.alloc_psum_tensor(f"ps_mm{i}", [128, 512], F32) for i in range(4)]
    ps_sc = nc.alloc_psum_tensor("ps_sc", [128, 512], F32)
    ps_o = [nc.alloc_psum_tensor(f"ps_o{i}", [128, 512], F32) for i in range(2)]
    ps_tr = nc.alloc_psum_tensor("ps_tr", [128, 512], F32)
    trb = [ps_tr, ps_o[0], ps_o[1]]

    S = {}
    for name in [
        "ini", "prep", "init0", "lna", "var", "rstd", "nmr", "xn",
        "trp", "trev", "mmp",
        "ek", "kfs", "vfms", "qss", "agn", "aarg", "aexp", "aprep",
        "qsd", "scj", "stl", "trk", "khe", "umm", "ssc", "omm", "oev",
        "aring", "mmgu", "silu", "hcmb",
        "mmdn", "dnev", "trq", "trqe",
        "pack", "dtot", "ccw", "cc", "ccl", "fold", "ea", "scorr",
        "fmm", "fsum", "cev",
        "xld0", "xld1", "wldp0", "wldp1", "wldgu0", "wldgu1",
        "wldd0", "wldd1", "dnpld0", "dnpld1", "dnpsp0", "dnpsp1",
        "osp0", "osp1", "qhsp0", "qhsp1", "dnsp0", "dnsp1",
        "fqh0", "fqh1", "fqh2", "fqh3", "cacc0", "cacc1", "fx0", "fx1", "facc0", "facc1",
        "facc2", "facc3",
        "outs0", "outs1", "outs2", "outs3", "vv", "av",
    ]:
        S[name] = nc.alloc_semaphore("sem_" + name)

    def P(fam, idx):
        return S[f"{fam}{idx % 2}"]

    PROJ = ([("k", oc, j) for oc in range(4) for j in range(2)]
            + [("v", oc, j) for oc in range(4) for j in range(2)]
            + [("q", oc, j) for oc in range(16) for j in range(2)])

    ORDER = []
    for tau in range(NT):
        for pg in range(48):
            ORDER.append(("proj", tau, pg))
        for it in PLANS[tau]:
            if it[0] in ("gu", "dn", "tr"):
                ORDER.append((tau,) + it)
    for _uq in range(64):
        ORDER.append((1, "corr", _uq))
    G2 = {k: i for i, k in enumerate(ORDER)}

    def gid(key):
        return G2[key]

    evrec = {}
    for key in ORDER:
        if key[0] == "proj":
            _, tau, pg = key
            fam, oc, j = PROJ[pg]
            if fam == "k":
                evrec[G2[key]] = ("ek", tau * 8 + oc * 2 + j + 1)
            elif fam == "v":
                evrec[G2[key]] = ("vfms", tau * 8 + oc * 2 + j + 1)
            else:
                evrec[G2[key]] = ("qss", tau * 32 + (pg - 16) + 1)
        else:
            tau = key[0]
            kind = key[1]
            if kind == "gu":
                _, _, part, icH, sub = key
                hc = tau * 88 + part * 22 + icH * 2 + (sub // 2)
                evrec[G2[key]] = ("silu" if sub % 2 == 0 else "hcmb", hc + 1)
            elif kind == "dn":
                _, _, part, ib = key
                evrec[G2[key]] = ("dnev", tau * 128 + part * 32 + ib + 1)
            elif kind == "tr":
                _, _, i = key
                evrec[G2[key]] = ("trqe", tau * 32 + i + 1)
            else:
                _, _, uq = key
                evrec[G2[key]] = ("cev", uq + 1)

    def bank_wait(eng, gidx):
        if gidx >= 4:
            semn, val = evrec[gidx - 4]
            eng.wait_ge(S[semn], val)

    def pslab_of(fam, oc):
        return {"k": 0, "v": 4, "q": 8}[fam] + oc

    def dnstg_free_wait(eng, tau, evg):
        pe = tau * 128 + evg - 2
        if pe < 0:
            return
        ptau, pevg = pe // 128, pe % 128
        pp, pib = pevg // 32, pevg % 32
        if pp < 3:
            spidx = ptau * 96 + pp * 32 + pib
            eng.wait_ge(P("dnpsp", spidx), 16 * (spidx // 2 + 1))
        else:
            eng.wait_ge(S["trq"], ptau * 32 + pib + 1)

    VV = [0]
    AV = [0]

    bg = nc.Block()
    blk = bg.__enter__()

    @blk.gpsimd
    def _(g):
        FSTEPS = []
        for fci in range(8):
            FSTEPS += [("x", fci), ("o", fci), ("dn", fci), ("st", fci)]
        for fci in range(8, 12):
            FSTEPS += [("x", fci), ("o", fci)]
        NPRE = len(FSTEPS)
        FSTEPS += [("acc", fci) for fci in range(4)]
        FSTEPS += [("dn", fci) for fci in range(8, 12)]
        FSTEPS += [("st", fci) for fci in range(8, 12)]
        FSTEPS += [("acc", fci) for fci in range(4, 8)]
        FSTEPS += [("x", fci) for fci in range(12, 16)]
        FSTEPS += [("o", fci) for fci in range(12, 16)]
        FSTEPS += [("dn", fci) for fci in range(12, 16)]
        FSTEPS += [("st", fci) for fci in range(12, 16)]
        FSTEPS += [("acc", fci) for fci in range(8, 16)]
        FPOS = [0]

        def emit_fstep(limit=None):
            if FPOS[0] >= (len(FSTEPS) if limit is None else limit):
                return
            knd, ci = FSTEPS[FPOS[0]]
            FPOS[0] += 1
            tau, c = ci // 8, ci % 8
            base = 64 * (ci // 4)
            fam = S[f"facc{ci % 4}"]
            slot = fsum[:, (ci % 4) * 2048:(ci % 4) * 2048 + 2048]
            rows = out_h[ci * 128:(ci + 1) * 128, :]
            if knd == "x":
                if ci == 0:
                    g.wait_ge(S["qsd"], 256)
                if ci >= 4:
                    g.wait_ge(fam, base)
                g.dma_start(out=slot, in_=x_h[ci * 128:(ci + 1) * 128, :]
                            ).then_inc(fam, 16)
            elif knd == "o":
                ug = tau * NU + c * 4 + 3
                g.wait_ge(P("osp", ug), 16 * (ug // 2 + 1))
                g.wait_ge(P("osp", ug - 1), 16 * ((ug - 1) // 2 + 1))
                g.wait_ge(fam, base + 16)
                g.dma_start(out=slot, in_=o_scr[ci, :, :],
                            accum_op=ALU.add).then_inc(fam, 16)
            elif knd == "dn":
                tq = tau * 32 + 30 + (c // 4)
                g.wait_ge(P("dnsp", tq), 16 * (tq // 2 + 1))
                g.wait_ge(fam, base + 32)
                g.dma_start(out=slot, in_=dn_scr[ci, :, :],
                            accum_op=ALU.add).then_inc(fam, 16)
            elif knd == "st":
                g.wait_ge(fam, base + 48)
                g.dma_start(out=rows, in_=slot).then_inc(fam, 16)
            else:
                g.wait_ge(S["cev"], 4 * (ci + 1))
                g.wait_ge(fam, base + 64)
                g.dma_start(out=rows,
                            in_=cstg[:, (ci % 2) * 2048:(ci % 2) * 2048 + 2048],
                            accum_op=ALU.add).then_inc(S[f"outs{ci % 4}"], 16)

        g.memset(ones_f[:, :], 1.0)
        g.memset(s_state[:, :], 0.0)
        g.memset(acs4[:, :], 0.0)
        g.memset(rfold[:, :], 0.0)
        g.memset(stil[:, :], 0)
        g.memset(epsb[:, :], 1e-5).then_inc(S["init0"], 1)

        for tau in range(NT):
            for ws in range(24):
                src, row0 = ((wk_h, ws) if ws < 4 else
                             (wv_h, ws - 4) if ws < 8 else (wq_h, ws - 8))
                wsg = tau * 24 + ws
                if wsg >= 2:
                    g.wait_ge(S["mmp"], (wsg - 2) * 2 + 2)
                if tau == 1 and ws < 2:
                    g.wait_ge(S["mmgu"], (42 + ws) * 4 + 4)
                for hh in range(2):
                    g.dma_start(
                        out=wun[:, (ws % 2) * 2048 + hh * 1024:(ws % 2) * 2048 + hh * 1024 + 1024],
                        in_=src[row0, :, hh * 1024:hh * 1024 + 1024],
                    ).then_inc(P("wldp", wsg), 16)
            for part in range(NPART):
                for icH in range(NICP):
                    gl = part * NICP + icH
                    icg = tau * 44 + gl
                    if gl < 2:
                        if tau == 0 and gl == 0:
                            g.wait_ge(S["mmp"], 48)
                        if tau == 1:
                            g.wait_ge(S["mmp"], 96)
                            g.wait_ge(S["mmgu"], (42 + gl) * 4 + 4)
                    else:
                        g.wait_ge(S["mmgu"], tau * 176 + (gl - 2) * 4 + 4)
                    g.dma_start(
                        out=wun[:, (gl % 2) * 4096:(gl % 2) * 4096 + 4096],
                        in_=wgu_h[part * NICP + icH, :, :],
                    ).then_inc(P("wldgu", icg), 16)
                    if tau == 1 and part >= 2:
                        emit_fstep(NPRE)
                for dmc in range(16):
                    dsl = tau * 64 + part * 16 + dmc
                    if dsl >= 2:
                        pd = dsl - 2
                        ptau, prest = pd // 64, pd % 64
                        ppart, pdmc = prest // 16, prest % 16
                        g.wait_ge(S["mmdn"],
                                  ptau * 128 + ppart * 32 + pdmc * 2 + 2)
                    g.dma_start(
                        out=wdbuf[:, (dmc % 2) * 1408:(dmc % 2) * 1408 + 1408],
                        in_=bass.AP(wd_h, part * NICP * 128 * 2048 + dmc * 128,
                                    [[2048, 128], [128 * 2048, NICP], [1, 128]]),
                    ).then_inc(P("wldd", dsl), 16)
                    if tau == 1 and part >= 2:
                        emit_fstep(NPRE)
                if tau == 1 and part == 1:
                    g.wait_ge(S["ccw"], 64)
                    g.collective_compute(
                        "AllGather", ALU.bypass,
                        replica_groups=[list(range(NCORE))],
                        ins=[cc_in.ap().opt()],
                        outs=[cc_out.ap().opt()],
                    ).then_inc(S["cc"], 1)

        while FPOS[0] < len(FSTEPS):
            emit_fstep()

    @blk.sync
    def _(s):
        for src, dstb in [(bqt_h, bqt), (bkt_h, bkt), (bvt_h, bvt),
                          (gst_h, gst), (ust_h, ust), (dst_h, dst),
                          (idn_h, ident), (msk_h, mask8), (cm_h, cmb),
                          (cm1_h, cm1b)]:
            s.dma_start(out=dstb[:, :], in_=src[:, :]).then_inc(S["ini"], 16)

        for tau in range(NT):
            for sub in range(NS):
                ts = tau * NS + sub
                if ts >= 2:
                    s.wait_ge(S["xn"], ts - 1)
                    s.wait_ge(S["lna"], ts - 1)
                if tau >= 1 and sub == 0:
                    s.wait_ge(S["mmdn"], 128 * tau)
                for hh in range(2):
                    s.dma_start(out=xbuf[:, (ts % 2) * 2048 + hh * 1024:(ts % 2) * 2048 + hh * 1024 + 1024],
                                in_=x_h[ts * 128:(ts + 1) * 128, hh * 1024:hh * 1024 + 1024]
                                ).then_inc(P("xld", ts), 16)

            for it in PLANS[tau]:
                kind = it[0]
                if kind == "attn":
                    u = it[1]
                    ug = tau * NU + u
                    ci = tau * NCH + u // 4
                    gi = u % 4
                    s.wait_ge(S["qsd"], ug * 4 + 1)
                    s.dma_start(out=qh_scr[ci, gi, :, :],
                                in_=qside[:, 0:512]).then_inc(P("qhsp", ug), 16)
                    s.wait_ge(S["oev"], ug + 1)
                    s.dma_start(out=bass.AP(o_scr, ci * 128 * 2048 + gi * 512,
                                            [[2048, 128], [1, 512]]),
                                in_=ostg[:, 0:512]).then_inc(P("osp", ug), 16)
                elif kind == "dn":
                    part, ib = it[1], it[2]
                    dmc, j = ib // 2, ib % 2
                    dnn = part * 32 + ib
                    if tau == 1 and part == 2 and ib < 8:
                        if ib == 0:
                            s.wait_ge(S["cc"], 1)
                            s.wait_ge(S["kfs"], 16)
                            s.wait_ge(S["aprep"], 64)
                            s.wait_ge(S["aring"], 64)
                        for r in range(ib * 4, ib * 4 + 4):
                            s.dma_start(out=ccst[:, r * 129:(r + 1) * 129],
                                        in_=cc_out[r, :, :]).then_inc(S["ccl"], 16)
                    ev_i = tau * 128 + dnn
                    if part < 3:
                        spidx = tau * 96 + dnn
                        s.wait_ge(S["dnev"], ev_i + 1)
                        s.dma_start(
                            out=bass.AP(dnp_scr, dmc * 128 * 2048 + tau * 1024 + j * 512,
                                        [[2048, 128], [1, 512]]),
                            in_=dnstg[:, (ev_i % 2) * 512:(ev_i % 2) * 512 + 512],
                        ).then_inc(P("dnpsp", spidx), 16)
                    rid_l = dnn - 30
                    if 0 <= rid_l < 96:
                        rid = tau * 96 + rid_l
                        rdmc, rj = (rid_l % 32) // 2, rid_l % 2
                        if rid_l >= 2:
                            s.wait_ge(S["dnev"], tau * 128 + rid_l + 31)
                        s.wait_ge(P("dnpsp", rid), 16 * (rid // 2 + 1))
                        s.dma_start(
                            out=dnpld[:, (rid % 2) * 512:(rid % 2) * 512 + 512],
                            in_=bass.AP(dnp_scr, rdmc * 128 * 2048 + tau * 1024 + rj * 512,
                                        [[2048, 128], [1, 512]]),
                        ).then_inc(P("dnpld", rid), 16)
                elif kind == "tr":
                    i = it[1]
                    tq = tau * 32 + i
                    dmc, j = i // 2, i % 2
                    s.wait_ge(S["trqe"], tq + 1)
                    s.dma_start(
                        out=bass.AP(dn_scr,
                                    (tau * 8 + j * 4) * 128 * 2048 + dmc * 128,
                                    [[2048, 128], [128 * 2048, 4], [1, 128]]),
                        in_=trstg[:, 0:512],
                    ).then_inc(P("dnsp", tq), 16)
                elif kind == "cc":
                    s.wait_ge(S["pack"], 1)
                    for gi in range(4):
                        s.dma_start(out=cc_in[gi, :, :],
                                    in_=spack[:, gi * 129:(gi + 1) * 129]
                                    ).then_inc(S["ccw"], 16)

        for pp in range(2):
            s.wait_ge(S[f"qhsp{pp}"], 16 * 32)
        for uq in range(64):
            if uq >= 4:
                s.wait_ge(S["fmm"], uq - 3)
            s.dma_start(out=qhld[:, (uq % 4) * 512:(uq % 4) * 512 + 512],
                        in_=qh_scr[uq // 4, uq % 4, :, :]
                        ).then_inc(S[f"fqh{uq % 4}"], 16)
        for ff in range(4):
            s.wait_ge(S[f"outs{ff}"], 16 * 4)

    @blk.tensor
    def _(t):
        t.wait_ge(S["ini"], 160)

        def emit_attn(u, tau):
            ug = tau * NU + u
            ci = u // 4
            gi = u % 4
            slot = ug % NRING
            t.wait_ge(S["aring"], ug)
            t.wait_ge(S["aprep"], ug + 1)
            t.wait_ge(S["trev"], (tau + 1) * 160)
            for j in range(4):
                t.wait_ge(S["qsd"], ug * 4 + j + 1)
                t.matmul(ps_sc[32 * j:32 * (j + 1), :],
                         kring[:, slot * 128 + 32 * j:slot * 128 + 32 * j + 32],
                         qside[:, j * 512:(j + 1) * 512],
                         start=True, stop=True,
                         tile_position=(0, 32 * j)).then_inc(S["scj"], 1)
            t.wait_ge(S["stl"], ug + 1)
            t.transpose(ps_sc[:, 0:128], khring[:, slot * 128:slot * 128 + 128],
                        ident[:, :]).then_inc(S["trk"], 1)
            t.wait_ge(S["khe"], ug + 1)
            t.matmul(ps_sc[:, 0:128],
                     khtb[:, (ug % 2) * 128:(ug % 2) * 128 + 128],
                     vtok[:, ci * 512 + gi * 128:ci * 512 + gi * 128 + 128],
                     start=True, stop=True).then_inc(S["umm"], 1)
            t.wait_ge(S["ssc"], ug + 1)
            if ug >= 2:
                t.wait_ge(S["oev"], ug - 1)
            last = None
            for hh in range(4):
                t.matmul(ps_o[ug % 2][:, hh * 128:(hh + 1) * 128],
                         stil[:, (ug % 2) * 512 + hh * 128:(ug % 2) * 512 + hh * 128 + 128],
                         vtok[:, ci * 512 + gi * 128:ci * 512 + gi * 128 + 128],
                         start=True, stop=False)
                last = t.matmul(ps_o[ug % 2][:, hh * 128:(hh + 1) * 128],
                                qside[:, hh * 128:hh * 128 + 128],
                                sscal[:, (ug % 2) * 128:(ug % 2) * 128 + 128],
                                start=False, stop=True)
            last.then_inc(S["omm"], 1)

        for tau in range(NT):
            for sub in range(NS):
                ts = tau * NS + sub
                t.wait_ge(S["xn"], ts + 1)
                for k in range(16):
                    trn = tau * 160 + sub * 16 + k
                    if trn >= 3:
                        t.wait_ge(S["trev"], trn - 2)
                    if tau >= 1 and sub == 0 and k == 0:
                        t.wait_ge(S["oev"], tau * NU)
                        t.wait_ge(S["aring"], tau * NU)
                    t.transpose(trb[trn % 3][:, 0:128],
                                xn_tok[:, k * 128:(k + 1) * 128],
                                ident[:, :]).then_inc(S["trp"], 1)
            for pg, (fam, oc, j) in enumerate(PROJ):
                gidx = gid(("proj", tau, pg))
                ws = pslab_of(fam, oc)
                wsg = tau * 24 + ws
                t.wait_ge(P("wldp", wsg), 32 * (wsg // 2 + 1))
                t.wait_ge(S["trev"], tau * 160 + 64 * (j + 1))
                bank_wait(t, gidx)
                bank = ps_mm[gidx % 4]
                last = None
                for kk in range(16):
                    last = t.matmul(
                        bank[:, :],
                        wun[:, (ws % 2) * 2048 + kk * 128:(ws % 2) * 2048 + kk * 128 + 128],
                        xnT[:, kk * 1024 + j * 512:kk * 1024 + j * 512 + 512],
                        start=(kk == 0), stop=(kk == 15))
                last.then_inc(S["mmp"], 1)
                if fam == "v":
                    vg = oc * 2 + j
                    for b in range(4):
                        trn = tau * 160 + 128 + vg * 4 + b
                        t.wait_ge(S["vfms"], tau * 8 + vg + 1)
                        if trn >= 3:
                            t.wait_ge(S["trev"], trn - 2)
                        t.transpose(trb[trn % 3][:, 0:128],
                                    vstg[:, (vg % 2) * 512 + b * 128:(vg % 2) * 512 + b * 128 + 128],
                                    ident[:, :]).then_inc(S["trp"], 1)
            for it in PLANS[tau]:
                kind = it[0]
                if kind == "attn":
                    emit_attn(it[1], tau)
                elif kind == "gu":
                    part, icH, sub = it[1], it[2], it[3]
                    gl = part * NICP + icH
                    icg = tau * 44 + gl
                    gu = sub % 2
                    j = sub // 2
                    if sub == 0:
                        t.wait_ge(P("wldgu", icg), 16 * (icg // 2 + 1))
                    gidx = gid((tau,) + it)
                    bank_wait(t, gidx)
                    bank = ps_mm[gidx % 4]
                    last = None
                    for kk in range(16):
                        last = t.matmul(
                            bank[:, :],
                            wun[:, (gl % 2) * 4096 + gu * 2048 + kk * 128:
                                 (gl % 2) * 4096 + gu * 2048 + kk * 128 + 128],
                            xnT[:, kk * 1024 + j * 512:kk * 1024 + j * 512 + 512],
                            start=(kk == 0), stop=(kk == 15))
                    last.then_inc(S["mmgu"], 1)
                elif kind == "dn":
                    part, ib = it[1], it[2]
                    dmc, j = ib // 2, ib % 2
                    dsl = tau * 64 + part * 16 + dmc
                    t.wait_ge(P("wldd", dsl), 16 * (dsl // 2 + 1))
                    t.wait_ge(S["hcmb"], tau * 88 + part * 22 + 21 + j)
                    gidx = gid((tau,) + it)
                    bank_wait(t, gidx)
                    bank = ps_mm[gidx % 4]
                    last = None
                    for icH in range(NICP):
                        last = t.matmul(
                            bank[:, :],
                            wdbuf[:, (dmc % 2) * 1408 + icH * 128:(dmc % 2) * 1408 + icH * 128 + 128],
                            hbuf[:, icH * 1024 + j * 512:icH * 1024 + j * 512 + 512],
                            start=(icH == 0), stop=(icH == NICP - 1))
                    last.then_inc(S["mmdn"], 1)
                elif kind == "tr":
                    i = it[1]
                    ev_i = tau * 128 + 96 + i
                    t.wait_ge(S["dnev"], ev_i + 1)
                    gidx = gid((tau,) + it)
                    bank_wait(t, gidx)
                    bank = ps_mm[gidx % 4]
                    last = None
                    for b in range(4):
                        last = t.transpose(
                            bank[:, b * 128:(b + 1) * 128],
                            dnstg[:, (ev_i % 2) * 512 + b * 128:(ev_i % 2) * 512 + b * 128 + 128],
                            ident[:, :])
                    last.then_inc(S["trq"], 1)

        for uq in range(64):
            t.wait_ge(S[f"fqh{uq % 4}"], 16 * (uq // 4 + 1))
            t.wait_ge(S["scorr"], uq + 1)
            gidx = gid((1, "corr", uq))
            bank_wait(t, gidx)
            bank = ps_mm[gidx % 4]
            last = None
            for hh in range(4):
                last = t.matmul(bank[:, hh * 128:(hh + 1) * 128],
                                qhld[:, (uq % 4) * 512 + hh * 128:(uq % 4) * 512 + hh * 128 + 128],
                                scorb[:, (uq % 4) * 128:(uq % 4) * 128 + 128],
                                start=True, stop=True)
            last.then_inc(S["fmm"], 1)

    @blk.scalar
    def _(a):
        a.wait_ge(S["ini"], 160)
        a.wait_ge(S["prep"], 1)

        def emit_burst(tau, c):
            if c + 1 < NCH:
                c2 = c + 1
                a.wait_ge(S["kfs"], tau * 8 + 7 + c2 // 4)
                a.wait_ge(S["aarg"], tau * NU + c2 * 4)
                a.activation(out=gbuf[:, :],
                             in_=bass.AP(kf, c2 * 128, [[4096, 128], [1024, 4], [1, 128]]),
                             func=AF.Ln, scale=-1.0,
                             bias=ones_f[:, 0:1]).then_inc(S["agn"], 1)
            C = tau * NCH + c
            bb = (C % 3) * 512
            for gg in range(4):
                ug = tau * NU + c * 4 + gg
                a.wait_ge(S["aarg"], ug + 1)
                if ug >= 4:
                    a.wait_ge(S["aprep"], ug - 3)
                if ug >= NRING:
                    a.wait_ge(S["aring"], ug - NRING + 1)
                slot = ug % NRING
                ks4 = (ug % 4) * 128
                bsl = bbuf3[:, bb + gg * 128:bb + gg * 128 + 128]
                for j in range(4):
                    a.activation(out=ering[:, slot * 512 + j * 128:slot * 512 + j * 128 + 128],
                                 in_=bsl, func=AF.Exp,
                                 bias=nb4r[:, slot * 4 + j:slot * 4 + j + 1])
                for j in range(4):
                    a.activation(out=kexpb[:, ks4 + 32 * j:ks4 + 32 * j + 32],
                                 in_=bbuf3[:, bb + gg * 128 + 32 * j:bb + gg * 128 + 32 * j + 32],
                                 func=AF.Exp, scale=-1.0,
                                 bias=bbuf3[:, bb + gg * 128 + 32 * j + 15:bb + gg * 128 + 32 * j + 16])
                a.activation(out=kh2eb[:, ks4:ks4 + 128], in_=bsl, func=AF.Exp,
                             scale=-1.0,
                             bias=bbuf3[:, bb + gg * 128 + 127:bb + gg * 128 + 128])
                a.activation(out=ebring[:, slot * 2:slot * 2 + 2],
                             in_=bass.AP(bbuf3, bb + gg * 128 + 15, [[3 * 512, 128], [112, 2]]),
                             func=AF.Exp).then_inc(S["aexp"], 1)

        for tau in range(NT):
            for sub in range(NS):
                ts = tau * NS + sub
                a.wait_ge(S["var"], ts + 1)
                AV[0] += 1
                a.activation(out=lnv[:, ts:ts + 1], in_=varb[:, ts:ts + 1],
                             func=AF.Ln, bias=epsb[:, 0:1]).then_inc(S["av"], 1)
                a.wait_ge(S["av"], AV[0])
                a.activation(out=rstd[:, ts:ts + 1], in_=lnv[:, ts:ts + 1],
                             func=AF.Exp, scale=-0.5).then_inc(S["rstd"], 1)
                a.wait_ge(S["nmr"], ts + 1)
                a.wait_ge(S["rstd"], ts + 1)
                if ts >= 1:
                    a.wait_ge(S["trp"], tau * 160 + sub * 16)
                a.activation(out=xn_tok[:, :],
                             in_=xbuf[:, (ts % 2) * 2048:(ts % 2) * 2048 + 2048],
                             func=AF.Identity, scale=rstd[:, ts:ts + 1],
                             bias=nmr[:, ts:ts + 1]).then_inc(S["xn"], 1)
            for pg, (fam, oc, j) in enumerate(PROJ):
                gidx = gid(("proj", tau, pg))
                bank = ps_mm[gidx % 4]
                a.wait_ge(S["mmp"], tau * 48 + pg + 1)
                if fam == "k":
                    kp = tau * 8 + oc * 2 + j
                    if kp >= 2:
                        a.wait_ge(S["kfs"], kp - 1)
                    a.activation(out=ekb[:, (kp % 2) * 512:(kp % 2) * 512 + 512],
                                 in_=bank[:, :], func=AF.Exp,
                                 scale=-1.0, bias=nbkt[:, oc:oc + 1]
                                 ).then_inc(S["ek"], 1)
                elif fam == "v":
                    vg = oc * 2 + j
                    if vg >= 2:
                        a.wait_ge(S["trp"], tau * 160 + 128 + 4 * vg - 4)
                    elif tau >= 1:
                        a.wait_ge(S["trp"], (tau - 1) * 160 + 128 + 4 * (vg + 6) + 4)
                    a.activation(out=vstg[:, (vg % 2) * 512:(vg % 2) * 512 + 512],
                                 in_=bank[:, :], func=AF.Identity,
                                 bias=bvt[:, oc:oc + 1]).then_inc(S["vfms"], 1)
                else:
                    if tau >= 1 and oc == 0 and j == 0:
                        a.wait_ge(S["qsd"], tau * 128)
                    a.activation(out=qs[:, oc * 1024 + j * 512:oc * 1024 + j * 512 + 512],
                                 in_=bank[:, :], func=AF.Relu, scale=SCALE,
                                 bias=bqs[:, oc:oc + 1]).then_inc(S["qss"], 1)
            a.wait_ge(S["kfs"], tau * 8 + 7)
            if tau >= 1:
                a.wait_ge(S["aarg"], tau * NU)
            a.activation(out=gbuf[:, :],
                         in_=bass.AP(kf, 0, [[4096, 128], [1024, 4], [1, 128]]),
                         func=AF.Ln, scale=-1.0,
                         bias=ones_f[:, 0:1]).then_inc(S["agn"], 1)
            emit_burst(tau, 0)
            for it in PLANS[tau]:
                kind = it[0]
                if kind == "attn" and it[1] % 4 == 0 and it[1] >= 4:
                    emit_burst(tau, it[1] // 4)
                elif kind == "cc":
                    a.wait_ge(S["aarg"], 2 * NU)
                    a.activation(out=dtot4[:, :], in_=acs4[:, :],
                                 func=AF.Exp).then_inc(S["dtot"], 1)
                    a.activation(out=eab[:, :], in_=bmacs[:, :],
                                 func=AF.Exp).then_inc(S["ea"], 1)
                elif kind == "gu" and it[3] % 2 == 0:
                    part, icH, sub = it[1], it[2], it[3]
                    j = sub // 2
                    ic = part * NICP + icH
                    hc = tau * 88 + part * 22 + icH * 2 + j
                    a.wait_ge(S["mmgu"], tau * 176 + ic * 4 + sub + 1)
                    if hc >= 2:
                        a.wait_ge(S["hcmb"], hc - 1)
                    a.activation(out=sgb[:, (j % 2) * 512:(j % 2) * 512 + 512],
                                 in_=ps_mm[gid((tau,) + it) % 4][:, :], func=AF.Silu,
                                 scale=gst[:, ic:ic + 1]).then_inc(S["silu"], 1)

    @blk.vector
    def _(v):
        v.wait_ge(S["ini"], 160)
        v.wait_ge(S["init0"], 1)
        v.tensor_scalar_mul(bqs[:, :], bqt[:, :], SCALE)
        v.tensor_scalar_mul(nbkt[:, :], bkt[:, :], -1.0).then_inc(S["prep"], 1)

        def emit_scan(c, tau):
            C = tau * NCH + c
            bb = (C % 3) * 512
            v.wait_ge(S["agn"], C + 1)
            if C >= 3:
                v.wait_ge(S["aexp"], 4 * C - 8)
            VV[0] += 1
            for g2 in range(4):
                last_scan = v.tensor_tensor_scan(
                    out=bbuf3[:, bb + g2 * 128:bb + (g2 + 1) * 128],
                    data0=ones_f[:, :],
                    data1=gbuf[:, g2 * 128:(g2 + 1) * 128],
                    initial=0.0, op0=ALU.mult, op1=ALU.add)
            last_scan.then_inc(S["vv"], 1)
            v.wait_ge(S["vv"], VV[0])

        def emit_args(u, tau):
            ug = tau * NU + u
            c = u // 4
            gg = u % 4
            C = tau * NCH + c
            bb = (C % 3) * 512
            if ug >= NRING:
                v.wait_ge(S["aexp"], ug - NRING + 1)
            if ug >= 4:
                v.wait_ge(S["aarg"], ug - 3)
            slot = ug % NRING
            v.tensor_scalar_mul(nb4r[:, slot * 4:slot * 4 + 4],
                                bass.AP(bbuf3, bb + gg * 128 + 15, [[3 * 512, 128], [32, 4]]),
                                -1.0)
            VV[0] += 1
            v.tensor_add(bmacs[:, ug:ug + 1], acs4[:, gg:gg + 1],
                         bbuf3[:, bb + gg * 128 + 15:bb + gg * 128 + 16]).then_inc(S["vv"], 1)
            v.wait_ge(S["vv"], VV[0])
            v.tensor_add(acs4[:, gg:gg + 1], acs4[:, gg:gg + 1],
                         bbuf3[:, bb + gg * 128 + 127:bb + gg * 128 + 128]).then_inc(S["aarg"], 1)

        def emit_attn_v(u, tau):
            ug = tau * NU + u
            c = u // 4
            gg = u % 4
            slot = ug % NRING
            if ug >= 4:
                v.wait_ge(S["aring"], ug - 3)
            v.wait_ge(S["aexp"], ug + 1)
            ks4 = (ug % 4) * 128
            kfsl = kf[:, gg * 1024 + c * 128:gg * 1024 + c * 128 + 128]
            v.tensor_mul(kring[:, slot * 128:(slot + 1) * 128], kfsl,
                         kexpb[:, ks4:ks4 + 128])
            v.tensor_mul(khring[:, slot * 128:(slot + 1) * 128], kfsl,
                         kh2eb[:, ks4:ks4 + 128]).then_inc(S["aprep"], 1)
            qsrc = bass.AP(qs, gg * 4 * 1024 + c * 128, [[16384, 128], [1024, 4], [1, 128]])
            v.wait_ge(S["qss"], tau * 32 + 8 * gg + 7 + c // 4)
            for j in range(4):
                if ug >= 1:
                    v.wait_ge(S["scj"], (ug - 1) * 4 + j + 1)
                    if j == 0:
                        v.wait_ge(S["omm"], ug)
                        v.wait_ge(P("qhsp", ug - 1), 16 * ((ug - 1) // 2 + 1))
                v.tensor_mul(
                    qside[:, j * 512:(j + 1) * 512],
                    qsrc,
                    bass.AP(ering, slot * 512 + j * 128, [[NRING * 512, 128], [0, 4], [1, 128]]),
                ).then_inc(S["qsd"], 1)
            v.tensor_scalar_mul(sscal[:, (ug % 2) * 128:(ug % 2) * 128 + 128],
                                s_state[:, gg * 128:(gg + 1) * 128],
                                ebring[:, slot * 2:slot * 2 + 1]).then_inc(S["ssc"], 1)
            v.wait_ge(S["scj"], ug * 4 + 4)
            v.copy_predicated(out=stil[:, (ug % 2) * 512:(ug % 2) * 512 + 512],
                              mask=mask8[:, :],
                              data=ps_sc[:, :]).then_inc(S["stl"], 1)
            v.wait_ge(S["trk"], ug + 1)
            v.tensor_copy(khtb[:, (ug % 2) * 128:(ug % 2) * 128 + 128],
                          ps_sc[:, 0:128]).then_inc(S["khe"], 1)
            v.wait_ge(S["umm"], ug + 1)
            v.scalar_tensor_tensor(
                out=s_state[:, gg * 128:(gg + 1) * 128],
                in0=s_state[:, gg * 128:(gg + 1) * 128],
                scalar=ebring[:, slot * 2 + 1:slot * 2 + 2],
                in1=ps_sc[:, 0:128],
                op0=ALU.mult, op1=ALU.add).then_inc(S["aring"], 1)
            v.wait_ge(S["omm"], ug + 1)
            if ug >= 1:
                v.wait_ge(P("osp", ug - 1), 16 * ((ug - 1) // 2 + 1))
            v.tensor_copy(ostg[:, 0:512],
                          ps_o[ug % 2][:, :]).then_inc(S["oev"], 1)
            w = u + 5
            if w < NU:
                if w % 4 == 0:
                    emit_scan(w // 4, tau)
                emit_args(w, tau)

        for tau in range(NT):
            for sub in range(NS + 1):
                if sub >= 1:
                    psub = sub - 1
                    for k in range(16):
                        trn = tau * 160 + psub * 16 + k
                        v.wait_ge(S["trp"], trn + 1)
                        if tau >= 1 and psub == 0 and k == 0:
                            v.wait_ge(S["mmgu"], tau * 176)
                        v.tensor_copy(
                            xnT[:, k * 1024 + psub * 128:k * 1024 + psub * 128 + 128],
                            trb[trn % 3][:, 0:128]).then_inc(S["trev"], 1)
                if sub < NS:
                    ts = tau * NS + sub
                    v.wait_ge(P("xld", ts), 32 * (ts // 2 + 1))
                    xb = xbuf[:, (ts % 2) * 2048:(ts % 2) * 2048 + 2048]
                    v.reduce_sum(stat1[:, ts:ts + 1], xb, axis=AX.X)
                    if tau >= 1 and sub == 0:
                        v.wait_ge(S["mmdn"], 128 * tau)
                    v.scalar_tensor_tensor(
                        out=sqscr[:, :], in0=xb, scalar=1.0, in1=xb,
                        op0=ALU.mult, op1=ALU.mult,
                        accum_out=stat2[:, ts:ts + 1]).then_inc(S["lna"], 1)
                    v.wait_ge(S["lna"], ts + 1)
                    v.tensor_scalar_mul(meanv[:, ts:ts + 1], stat1[:, ts:ts + 1], 1.0 / 2048)
                    VV[0] += 1
                    v.tensor_scalar_mul(em2[:, ts:ts + 1], stat2[:, ts:ts + 1], 1.0 / 2048).then_inc(S["vv"], 1)
                    v.wait_ge(S["vv"], VV[0])
                    VV[0] += 1
                    v.tensor_mul(varb[:, ts:ts + 1], meanv[:, ts:ts + 1], meanv[:, ts:ts + 1]).then_inc(S["vv"], 1)
                    v.wait_ge(S["vv"], VV[0])
                    v.tensor_sub(varb[:, ts:ts + 1], em2[:, ts:ts + 1], varb[:, ts:ts + 1]).then_inc(S["var"], 1)
                    v.wait_ge(S["rstd"], ts + 1)
                    v.scalar_tensor_tensor(
                        out=nmr[:, ts:ts + 1], in0=meanv[:, ts:ts + 1], scalar=-1.0,
                        in1=rstd[:, ts:ts + 1], op0=ALU.mult, op1=ALU.mult).then_inc(S["nmr"], 1)
            for oc in range(4):
                for j in range(2):
                    kp = tau * 8 + oc * 2 + j
                    v.wait_ge(S["ek"], kp + 1)
                    if tau >= 1 and oc == 0 and j == 0:
                        v.wait_ge(S["aprep"], tau * NU)
                    if kp >= 1:
                        v.wait_ge(S["kfs"], kp)
                    VV[0] += 1
                    v.tensor_scalar_add(kpost[:, :],
                                        ekb[:, (kp % 2) * 512:(kp % 2) * 512 + 512], 1.0).then_inc(S["vv"], 1)
                    v.wait_ge(S["vv"], VV[0])
                    VV[0] += 1
                    v.reciprocal(kpost2[:, :], kpost[:, :]).then_inc(S["vv"], 1)
                    v.wait_ge(S["vv"], VV[0])
                    v.tensor_scalar_min(
                        kf[:, oc * 1024 + j * 512:oc * 1024 + j * 512 + 512],
                        kpost2[:, :], 0.95).then_inc(S["kfs"], 1)
            for voc in range(4):
                for j in range(2):
                    for b in range(4):
                        vg = voc * 2 + j
                        trn = tau * 160 + 128 + vg * 4 + b
                        vs = j * 4 + b
                        v.wait_ge(S["trp"], trn + 1)
                        if tau >= 1 and vg == 0 and b == 0:
                            v.wait_ge(S["omm"], tau * NU)
                            v.wait_ge(S["umm"], tau * NU)
                        v.tensor_copy(
                            vtok[:, vs * 512 + voc * 128:vs * 512 + voc * 128 + 128],
                            trb[trn % 3][:, 0:128]).then_inc(S["trev"], 1)
            emit_scan(0, tau)
            for u0 in range(4):
                emit_args(u0, tau)
            emit_scan(1, tau)
            emit_args(4, tau)
            for it in PLANS[tau]:
                kind = it[0]
                if kind == "attn":
                    emit_attn_v(it[1], tau)
                elif kind == "gu" and it[3] % 2 == 1:
                    part, icH, sub = it[1], it[2], it[3]
                    j = sub // 2
                    ic = part * NICP + icH
                    hc = tau * 88 + part * 22 + icH * 2 + j
                    v.wait_ge(S["mmgu"], tau * 176 + ic * 4 + sub + 1)
                    v.wait_ge(S["silu"], hc + 1)
                    if part == 0 and icH == 0 and j == 0:
                        v.wait_ge(S["trp"], tau * 160 + 128)
                        v.wait_ge(S["xn"], 8 * (tau + 1))
                        v.wait_ge(S["lna"], 8 * (tau + 1))
                        v.wait_ge(S["mmdn"], 128 * tau)
                    v.scalar_tensor_tensor(
                        out=hbuf[:, icH * 1024 + j * 512:icH * 1024 + j * 512 + 512],
                        in0=ps_mm[gid((tau,) + it) % 4][:, :],
                        scalar=ust[:, ic:ic + 1],
                        in1=sgb[:, (j % 2) * 512:(j % 2) * 512 + 512],
                        op0=ALU.mult, op1=ALU.mult).then_inc(S["hcmb"], 1)
                elif kind == "dn":
                    part, ib = it[1], it[2]
                    dmc, j = ib // 2, ib % 2
                    ev_i = tau * 128 + part * 32 + ib
                    evg = part * 32 + ib
                    v.wait_ge(S["mmdn"], ev_i + 1)
                    dnstg_free_wait(v, tau, evg)
                    if part == 0:
                        v.tensor_scalar_mul(
                            dnstg[:, (ev_i % 2) * 512:(ev_i % 2) * 512 + 512],
                            ps_mm[gid((tau,) + it) % 4][:, :],
                            dst[:, dmc:dmc + 1]).then_inc(S["dnev"], 1)
                    else:
                        rid = tau * 96 + (part - 1) * 32 + ib
                        v.wait_ge(P("dnpld", rid), 16 * (rid // 2 + 1))
                        v.scalar_tensor_tensor(
                            out=dnstg[:, (ev_i % 2) * 512:(ev_i % 2) * 512 + 512],
                            in0=ps_mm[gid((tau,) + it) % 4][:, :],
                            scalar=dst[:, dmc:dmc + 1],
                            in1=dnpld[:, (rid % 2) * 512:(rid % 2) * 512 + 512],
                            op0=ALU.mult, op1=ALU.add).then_inc(S["dnev"], 1)
                elif kind == "tr":
                    i = it[1]
                    tq = tau * 32 + i
                    v.wait_ge(S["trq"], tq + 1)
                    if tq >= 1:
                        v.wait_ge(P("dnsp", tq - 1), 16 * ((tq - 1) // 2 + 1))
                    trg = gid((tau, "tr", i))
                    v.tensor_copy(trstg[:, 0:512],
                                  ps_mm[trg % 4][:, :]).then_inc(S["trqe"], 1)
                elif kind == "cc":
                    v.wait_ge(S["aring"], 2 * NU)
                    v.wait_ge(S["dtot"], 1)
                    pk = None
                    for gg in range(4):
                        v.tensor_copy(spack[:, gg * 129:gg * 129 + 1], dtot4[:, gg:gg + 1])
                        pk = v.tensor_copy(spack[:, gg * 129 + 1:(gg + 1) * 129],
                                           s_state[:, gg * 128:(gg + 1) * 128])
                    pk.then_inc(S["pack"], 1)

        v.wait_ge(S["ccl"], 16 * 32)
        for jj in range(8):
            ii = None
            for gg in range(4):
                r = jj * 4 + gg
                ii = v.tensor_scalar(dpcol4[:, gg:gg + 1],
                                     ccst[:, r * 129:r * 129 + 1],
                                     cmb[:, jj:jj + 1],
                                     cm1b[:, jj:jj + 1], ALU.mult, ALU.add)
            VV[0] += 1
            ii.then_inc(S["vv"], 1)
            v.wait_ge(S["vv"], VV[0])
            for gg in range(4):
                ii = v.tensor_scalar(rtmp4[:, gg * 129:(gg + 1) * 129],
                                     rfold[:, gg * 129:(gg + 1) * 129],
                                     dpcol4[:, gg:gg + 1], None, ALU.mult)
            VV[0] += 1
            ii.then_inc(S["vv"], 1)
            v.wait_ge(S["vv"], VV[0])
            for gg in range(4):
                r = jj * 4 + gg
                ii = v.scalar_tensor_tensor(
                    out=rfold[:, gg * 129:(gg + 1) * 129],
                    in0=ccst[:, r * 129:(r + 1) * 129],
                    scalar=cmb[:, jj:jj + 1],
                    in1=rtmp4[:, gg * 129:(gg + 1) * 129],
                    op0=ALU.mult, op1=ALU.add)
            VV[0] += 1
            ii.then_inc(S["vv"], 1)
            v.wait_ge(S["vv"], VV[0])
        v.memset(dpcol4[:, 0:1], 0.0).then_inc(S["fold"], 1)
        v.wait_ge(S["ea"], 1)

        def emit_cevac(cq):
            v.wait_ge(S["fmm"], cq + 1)
            if cq >= 8:
                oc = cq // 4 - 2
                v.wait_ge(S[f"outs{oc % 4}"], 16 * (oc // 4 + 1))
            v.tensor_copy(cstg[:, (cq % 8) * 512:(cq % 8) * 512 + 512],
                          ps_mm[gid((1, "corr", cq)) % 4][:, :]
                          ).then_inc(S["cev"], 1)

        def emit_scorr(uq):
            v.tensor_scalar_mul(
                scorb[:, (uq % 4) * 128:(uq % 4) * 128 + 128],
                rfold[:, (uq % 4) * 129 + 1:(uq % 4) * 129 + 129],
                eab[:, uq:uq + 1]).then_inc(S["scorr"], 1)

        for uq in range(4):
            emit_scorr(uq)
        for uq in range(64):
            emit_cevac(uq)
            if uq + 4 < 64:
                emit_scorr(uq + 4)

    bg.__exit__(None, None, None)
    return nc


_NC_CACHE = {}


def _get_nc():
    if "nc" not in _NC_CACHE:
        _NC_CACHE["nc"] = build_nc()
    return _NC_CACHE["nc"]


def _prep_inputs(kw):
    f = np.float32
    x = np.asarray(kw["L0_attn_hidden_states"], f).reshape(16384, 2048)
    Wq = np.asarray(kw["L0_attn_Wq"], f)
    Wk = np.asarray(kw["L0_attn_Wk"], f)
    Wv = np.asarray(kw["L0_attn_Wv"], f)
    gw = np.asarray(kw["L0_ffn_gate_w"], f)
    uw = np.asarray(kw["L0_ffn_up_w"], f)
    dw = np.asarray(kw["L0_ffn_down_w"], f)

    def proj_slabs(W, noc):
        a = W.reshape(noc, 128, 16, 128).transpose(0, 3, 2, 1)
        return np.ascontiguousarray(a.reshape(noc, 128, 2048))

    wq_s = proj_slabs(Wq, 16)
    wk_s = proj_slabs(Wk, 4)
    wv_s = proj_slabs(Wv, 4)

    def ffn_slab(W):
        return W.reshape(44, 128, 16, 128).transpose(0, 3, 2, 1)

    gs_ = ffn_slab(gw)
    us_ = ffn_slab(uw)
    wgu_s = np.ascontiguousarray(
        np.stack([gs_, us_], axis=2).reshape(44, 128, 4096))
    wd_s = np.ascontiguousarray(dw.T.reshape(44, 128, 2048))

    def colmajor(vec, n):
        return np.ascontiguousarray(np.asarray(vec, f).reshape(n, 128).T)

    base = {
        "wq_s": wq_s, "wk_s": wk_s, "wv_s": wv_s, "wgu_s": wgu_s, "wd_s": wd_s,
        "bqt": colmajor(kw["L0_attn_bq"], 16),
        "bkt": colmajor(kw["L0_attn_bk"], 4),
        "bvt": colmajor(kw["L0_attn_bv"], 4),
        "gst": colmajor(kw["L0_ffn_gate_s"], 44),
        "ust": colmajor(kw["L0_ffn_up_s"], 44),
        "dst": colmajor(kw["L0_ffn_down_s"], 16),
        "identf": np.eye(128, dtype=f),
        "mask8": np.ascontiguousarray(
            (np.arange(128)[None, :] >= np.arange(128)[:, None])[:, None, :]
            .repeat(4, axis=1).reshape(128, 512).astype(np.int8)),
    }
    maps = []
    for core in range(NCORE):
        m = dict(base)
        m["x"] = np.ascontiguousarray(x[core * 2048:(core + 1) * 2048])
        cmv = np.zeros((128, 8), f)
        cmv[:, :core] = 1.0
        m["cmask"] = cmv
        cm1 = np.zeros((128, 8), f)
        cm1[:, core:] = 1.0
        m["cmask1"] = cm1
        maps.append(m)
    return maps


def kernel(**inputs):
    nc = _get_nc()
    in_maps = _prep_inputs(inputs)
    res = run_bass_kernel_spmd(nc, in_maps, core_ids=list(range(NCORE)))
    out = np.concatenate([res.results[i]["out"] for i in range(NCORE)], axis=0)
    return out.reshape(1, 16384, 2048).astype(np.float32)


# version 12
# speedup vs baseline: 1.0156x; 1.0156x over previous
import numpy as np
import ml_dtypes
import concourse.bass as bass
import concourse.mybir as mybir
import concourse.bass_utils as _bu
from concourse.bass_utils import run_bass_kernel_spmd

F32 = mybir.dt.float32
BF16 = mybir.dt.bfloat16
I8 = mybir.dt.int8
AF = mybir.ActivationFunctionType
ALU = mybir.AluOpType
AX = mybir.AxisListType

NCORE = 8
TCORE = 2048
NT = 2
TT = 1024
NS = 8
NCH = 8
NU = 32
NRING = 8
NPART = 4
NICP = 11
SCALE = 0.08838834764831845
DM = 2048


def build_plan(tau):
    items = []
    for part in range(NPART):
        for icH in range(NICP):
            for sub in range(4):
                items.append(("gu", part, icH, sub))
        if part < 3:
            for ib in range(32):
                items.append(("dn", part, ib))
        else:
            for i in range(32):
                items.append(("dn", part, i))
                if i >= 1:
                    items.append(("tr", i - 1))
            items.append(("tr", 31))
    merged = []
    ai = 0
    ci = 0
    for idx, it in enumerate(items):
        while ai < NU and idx >= int(1.5 * ai):
            merged.append(("attn", ai))
            ai += 1
        if tau == 1 and idx == 97:
            merged.append(("cc",))
        merged.append(it)
    while ai < NU:
        merged.append(("attn", ai))
        ai += 1
    return merged


PLANS = [build_plan(0), build_plan(1)]

GIDX = {}
_g = 0
for _tau in range(NT):
    for _it in PLANS[_tau]:
        if _it[0] in ("gu", "dn", "tr", "corr"):
            GIDX[(_tau,) + _it] = _g
            _g += 1
NGRP = _g


def build_nc():
    nc = bass.Bass("TRN2", target_bir_lowering=False)

    x_h = nc.dram_tensor("x", [TCORE, DM], F32, kind="ExternalInput")
    wq_h = nc.dram_tensor("wq_s", [16, 128, 2048], F32, kind="ExternalInput")
    wk_h = nc.dram_tensor("wk_s", [4, 128, 2048], F32, kind="ExternalInput")
    wv_h = nc.dram_tensor("wv_s", [4, 128, 2048], F32, kind="ExternalInput")
    wgu_h = nc.dram_tensor("wgu_s", [44, 128, 4096], F32, kind="ExternalInput")
    wd_h = nc.dram_tensor("wd_s", [44, 128, 2048], F32, kind="ExternalInput")
    bqt_h = nc.dram_tensor("bqt", [128, 16], F32, kind="ExternalInput")
    bkt_h = nc.dram_tensor("bkt", [128, 4], F32, kind="ExternalInput")
    bvt_h = nc.dram_tensor("bvt", [128, 4], F32, kind="ExternalInput")
    gst_h = nc.dram_tensor("gst", [128, 44], F32, kind="ExternalInput")
    ust_h = nc.dram_tensor("ust", [128, 44], F32, kind="ExternalInput")
    dst_h = nc.dram_tensor("dst", [128, 16], F32, kind="ExternalInput")
    idn_h = nc.dram_tensor("identf", [128, 128], F32, kind="ExternalInput")
    msk_h = nc.dram_tensor("mask8", [128, 512], I8, kind="ExternalInput")
    cm_h = nc.dram_tensor("cmask", [128, 8], F32, kind="ExternalInput")
    cm1_h = nc.dram_tensor("cmask1", [128, 8], F32, kind="ExternalInput")

    out_h = nc.dram_tensor("out", [TCORE, DM], F32, kind="ExternalOutput")

    o_scr = nc.dram_tensor("o_scr", [16, 128, 2048], F32, kind="Internal")
    qh_scr = nc.dram_tensor("qh_scr", [16, 4, 128, 512], BF16, kind="Internal")
    dnp_scr = nc.dram_tensor("dnp_scr", [16, 128, 2048], F32, kind="Internal")
    dn_scr = nc.dram_tensor("dn_scr", [16, 128, 2048], F32, kind="Internal")
    cc_in = nc.dram_tensor("cc_in", [4, 128, 129], F32, kind="Internal")
    cc_out = nc.dram_tensor("cc_out", [32, 128, 129], F32, kind="Internal")

    ones_f = nc.alloc_sbuf_tensor("ones_f", [128, 128], F32)
    ident = nc.alloc_sbuf_tensor("ident", [128, 128], F32)
    mask8 = nc.alloc_sbuf_tensor("mask8s", [128, 512], I8)
    cmb = nc.alloc_sbuf_tensor("cmb", [128, 8], F32)
    cm1b = nc.alloc_sbuf_tensor("cm1b", [128, 8], F32)
    bqt = nc.alloc_sbuf_tensor("bqt_s", [128, 16], F32)
    bqs = nc.alloc_sbuf_tensor("bqs_s", [128, 16], F32)
    bkt = nc.alloc_sbuf_tensor("bkt_s", [128, 4], F32)
    nbkt = nc.alloc_sbuf_tensor("nbkt_s", [128, 4], F32)
    bvt = nc.alloc_sbuf_tensor("bvt_s", [128, 4], F32)
    gst = nc.alloc_sbuf_tensor("gst_s", [128, 44], F32)
    ust = nc.alloc_sbuf_tensor("ust_s", [128, 44], F32)
    dst = nc.alloc_sbuf_tensor("dst_s", [128, 16], F32)
    stat1 = nc.alloc_sbuf_tensor("stat1", [128, 16], F32)
    stat2 = nc.alloc_sbuf_tensor("stat2", [128, 16], F32)
    meanv = nc.alloc_sbuf_tensor("meanv", [128, 16], F32)
    em2 = nc.alloc_sbuf_tensor("em2", [128, 16], F32)
    varb = nc.alloc_sbuf_tensor("varb", [128, 16], F32)
    lnv = nc.alloc_sbuf_tensor("lnv", [128, 16], F32)
    rstd = nc.alloc_sbuf_tensor("rstd", [128, 16], F32)
    nmr = nc.alloc_sbuf_tensor("nmr", [128, 16], F32)
    acs4 = nc.alloc_sbuf_tensor("acs4", [128, 4], F32)
    bmacs = nc.alloc_sbuf_tensor("bmacs", [128, 64], F32)
    dtot4 = nc.alloc_sbuf_tensor("dtot4", [128, 4], F32)
    eab = nc.alloc_sbuf_tensor("eab", [128, 64], F32)
    s_state = nc.alloc_sbuf_tensor("s_state", [128, 4 * 128], F32)
    spack = nc.alloc_sbuf_tensor("spack", [128, 4 * 129], F32)
    rfold = nc.alloc_sbuf_tensor("rfold", [128, 4 * 129], F32)
    dpcol4 = nc.alloc_sbuf_tensor("dpcol4", [128, 4], F32)
    epsb = nc.alloc_sbuf_tensor("epsb", [128, 1], F32)
    ebring = nc.alloc_sbuf_tensor("ebring", [128, NRING * 2], F32)
    ering = nc.alloc_sbuf_tensor("ering", [128, NRING * 512], BF16)
    kring = nc.alloc_sbuf_tensor("kring", [128, NRING * 128], BF16)
    khring = nc.alloc_sbuf_tensor("khring", [128, NRING * 128], F32)
    gbuf = nc.alloc_sbuf_tensor("gbuf", [128, 512], F32)
    bbuf3 = nc.alloc_sbuf_tensor("bbuf3", [128, 3 * 512], F32)
    nb4r = nc.alloc_sbuf_tensor("nb4r", [128, NRING * 4], F32)
    kexpb = nc.alloc_sbuf_tensor("kexpb", [128, 4 * 128], F32)
    kh2eb = nc.alloc_sbuf_tensor("kh2eb", [128, 4 * 128], F32)
    ekb = nc.alloc_sbuf_tensor("ekb", [128, 2 * 512], F32)
    kpost = nc.alloc_sbuf_tensor("kpost", [128, 512], F32)
    kpost2 = nc.alloc_sbuf_tensor("kpost2", [128, 512], F32)
    rtmp4 = nc.alloc_sbuf_tensor("rtmp4", [128, 4 * 129], F32)
    qside = nc.alloc_sbuf_tensor("qside", [128, 4 * 512], BF16)
    stil = nc.alloc_sbuf_tensor("stil", [128, 2 * 512], BF16)
    khtb = nc.alloc_sbuf_tensor("khtb", [128, 2 * 128], BF16)
    sscal = nc.alloc_sbuf_tensor("sscal", [128, 2 * 128], BF16)
    scorb = nc.alloc_sbuf_tensor("scorb", [128, 4 * 128], BF16)
    corrstg = nc.alloc_sbuf_tensor("corrstg", [128, 512], F32)
    ostg = nc.alloc_sbuf_tensor("ostg", [128, 512], F32)
    sgb = nc.alloc_sbuf_tensor("sgb", [128, 2 * 512], BF16)
    dnstg = nc.alloc_sbuf_tensor("dnstg", [128, 2 * 512], F32)
    dnpld = nc.alloc_sbuf_tensor("dnpld", [128, 2 * 512], F32)
    trstg = nc.alloc_sbuf_tensor("trstg", [128, 512], F32)
    vstg = nc.alloc_sbuf_tensor("vstg", [128, 2 * 512], F32)
    wdbuf = nc.alloc_sbuf_tensor("wdbuf", [128, 2 * 1408], BF16)
    wun = nc.alloc_sbuf_tensor("wun", [128, 8192], BF16)

    AR = (nc.sbuf_base + 63) & ~63
    assert AR + 112 * 1024 <= nc.sbuf_top, (AR, nc.sbuf_top)
    xnT = nc.alloc_sbuf_tensor_at("xnT", [128, 16 * 1024], BF16, offset=AR + 0)
    qs = nc.alloc_sbuf_tensor_at("qs", [128, 16 * 1024], BF16, offset=AR + 32 * 1024)
    kf = nc.alloc_sbuf_tensor_at("kf", [128, 4 * 1024], F32, offset=AR + 64 * 1024)
    vtok = nc.alloc_sbuf_tensor_at("vtok", [128, 8 * 512], BF16, offset=AR + 80 * 1024)
    qhld = nc.alloc_sbuf_tensor_at("qhld", [128, 4 * 512], BF16, offset=AR + 84 * 1024)
    hbuf = nc.alloc_sbuf_tensor_at("hbuf", [128, NICP * 1024], BF16, offset=AR + 88 * 1024)
    xn_tok = nc.alloc_sbuf_tensor_at("xn_tok", [128, 2048], F32, offset=AR + 88 * 1024)
    xbuf = nc.alloc_sbuf_tensor_at("xbuf", [128, 2 * 2048], F32, offset=AR + 96 * 1024)
    fsum = nc.alloc_sbuf_tensor_at("fsum", [128, 4 * 2048], F32, offset=AR + 32 * 1024)
    sqscr = nc.alloc_sbuf_tensor_at("sqscr", [128, 2048], F32, offset=AR + 32 * 1024)
    cstg = nc.alloc_sbuf_tensor_at("cstg", [128, 8 * 512], F32, offset=AR + 0)
    ccst = nc.alloc_sbuf_tensor_at("ccst", [128, 32 * 129], F32, offset=AR + 64 * 1024)

    ps_mm = [nc.alloc_psum_tensor(f"ps_mm{i}", [128, 512], F32) for i in range(4)]
    ps_sc = nc.alloc_psum_tensor("ps_sc", [128, 512], F32)
    ps_o = [nc.alloc_psum_tensor(f"ps_o{i}", [128, 512], F32) for i in range(2)]
    ps_tr = nc.alloc_psum_tensor("ps_tr", [128, 512], F32)
    trb = [ps_tr, ps_o[0], ps_o[1]]

    S = {}
    for name in [
        "ini", "prep", "init0", "lna", "var", "rstd", "nmr", "xn",
        "trp", "trev", "mmp",
        "ek", "kfs", "vfms", "qss", "agn", "aarg", "aexp", "aprep",
        "qsd", "scj", "stl", "trk", "khe", "umm", "ssc", "omm", "oev",
        "aring", "mmgu", "silu", "hcmb",
        "mmdn", "dnev", "trq", "trqe",
        "pack", "dtot", "ccw", "cc", "ccl", "fold", "ea", "scorr",
        "fmm", "fsum", "cev",
        "xld0", "xld1", "wldp0", "wldp1", "wldgu0", "wldgu1",
        "wldd0", "wldd1", "dnpld0", "dnpld1", "dnpsp0", "dnpsp1",
        "osp0", "osp1", "qhsp0", "qhsp1", "dnsp0", "dnsp1",
        "fqh0", "fqh1", "fqh2", "fqh3", "cacc0", "cacc1", "fx0", "fx1", "facc0", "facc1",
        "facc2", "facc3",
        "outs0", "outs1", "outs2", "outs3", "vv", "av",
    ]:
        S[name] = nc.alloc_semaphore("sem_" + name)

    def P(fam, idx):
        return S[f"{fam}{idx % 2}"]

    PROJ = ([("k", oc, j) for oc in range(4) for j in range(2)]
            + [("v", oc, j) for oc in range(4) for j in range(2)]
            + [("q", oc, j) for oc in range(16) for j in range(2)])

    ORDER = []
    for tau in range(NT):
        for pg in range(48):
            ORDER.append(("proj", tau, pg))
        for it in PLANS[tau]:
            if it[0] in ("gu", "dn", "tr"):
                ORDER.append((tau,) + it)
    for _uq in range(64):
        ORDER.append((1, "corr", _uq))
    G2 = {k: i for i, k in enumerate(ORDER)}

    def gid(key):
        return G2[key]

    evrec = {}
    for key in ORDER:
        if key[0] == "proj":
            _, tau, pg = key
            fam, oc, j = PROJ[pg]
            if fam == "k":
                evrec[G2[key]] = ("ek", tau * 8 + oc * 2 + j + 1)
            elif fam == "v":
                evrec[G2[key]] = ("vfms", tau * 8 + oc * 2 + j + 1)
            else:
                evrec[G2[key]] = ("qss", tau * 32 + (pg - 16) + 1)
        else:
            tau = key[0]
            kind = key[1]
            if kind == "gu":
                _, _, part, icH, sub = key
                hc = tau * 88 + part * 22 + icH * 2 + (sub // 2)
                evrec[G2[key]] = ("silu" if sub % 2 == 0 else "hcmb", hc + 1)
            elif kind == "dn":
                _, _, part, ib = key
                evrec[G2[key]] = ("dnev", tau * 128 + part * 32 + ib + 1)
            elif kind == "tr":
                _, _, i = key
                evrec[G2[key]] = ("trqe", tau * 32 + i + 1)
            else:
                _, _, uq = key
                evrec[G2[key]] = ("cev", uq + 1)

    def bank_wait(eng, gidx):
        if gidx >= 4:
            semn, val = evrec[gidx - 4]
            eng.wait_ge(S[semn], val)

    def pslab_of(fam, oc):
        return {"k": 0, "v": 4, "q": 8}[fam] + oc

    def dnstg_free_wait(eng, tau, evg):
        pe = tau * 128 + evg - 2
        if pe < 0:
            return
        ptau, pevg = pe // 128, pe % 128
        pp, pib = pevg // 32, pevg % 32
        if pp < 3:
            spidx = ptau * 96 + pp * 32 + pib
            eng.wait_ge(P("dnpsp", spidx), 16 * (spidx // 2 + 1))
        else:
            eng.wait_ge(S["trq"], ptau * 32 + pib + 1)

    VV = [0]
    AV = [0]

    bg = nc.Block()
    blk = bg.__enter__()

    @blk.gpsimd
    def _(g):
        FSTEPS = []
        for fci in range(8):
            FSTEPS += [("x", fci), ("o", fci), ("dn", fci), ("st", fci)]
        for fci in range(8, 12):
            FSTEPS += [("x", fci), ("o", fci)]
        NPRE = len(FSTEPS)
        FSTEPS += [("acc", fci) for fci in range(4)]
        FSTEPS += [("dn", fci) for fci in range(8, 12)]
        FSTEPS += [("st", fci) for fci in range(8, 12)]
        FSTEPS += [("acc", fci) for fci in range(4, 8)]
        FSTEPS += [("x", fci) for fci in range(12, 16)]
        FSTEPS += [("o", fci) for fci in range(12, 16)]
        FSTEPS += [("dn", fci) for fci in range(12, 16)]
        FSTEPS += [("st", fci) for fci in range(12, 16)]
        FSTEPS += [("acc", fci) for fci in range(8, 16)]
        FPOS = [0]

        def emit_fstep(limit=None):
            if FPOS[0] >= (len(FSTEPS) if limit is None else limit):
                return
            knd, ci = FSTEPS[FPOS[0]]
            FPOS[0] += 1
            tau, c = ci // 8, ci % 8
            base = 64 * (ci // 4)
            fam = S[f"facc{ci % 4}"]
            slot = fsum[:, (ci % 4) * 2048:(ci % 4) * 2048 + 2048]
            rows = out_h[ci * 128:(ci + 1) * 128, :]
            if knd == "x":
                if ci == 0:
                    g.wait_ge(S["qsd"], 256)
                if ci >= 4:
                    g.wait_ge(fam, base)
                g.dma_start(out=slot, in_=x_h[ci * 128:(ci + 1) * 128, :]
                            ).then_inc(fam, 16)
            elif knd == "o":
                ug = tau * NU + c * 4 + 3
                g.wait_ge(P("osp", ug), 16 * (ug // 2 + 1))
                g.wait_ge(P("osp", ug - 1), 16 * ((ug - 1) // 2 + 1))
                g.wait_ge(fam, base + 16)
                g.dma_start(out=slot, in_=o_scr[ci, :, :],
                            accum_op=ALU.add).then_inc(fam, 16)
            elif knd == "dn":
                tq = tau * 32 + 30 + (c // 4)
                g.wait_ge(P("dnsp", tq), 16 * (tq // 2 + 1))
                g.wait_ge(fam, base + 32)
                g.dma_start(out=slot, in_=dn_scr[ci, :, :],
                            accum_op=ALU.add).then_inc(fam, 16)
            elif knd == "st":
                g.wait_ge(fam, base + 48)
                g.dma_start(out=rows, in_=slot).then_inc(fam, 16)
            else:
                g.wait_ge(S["cev"], 4 * (ci + 1))
                g.wait_ge(fam, base + 64)
                g.dma_start(out=rows,
                            in_=cstg[:, (ci % 2) * 2048:(ci % 2) * 2048 + 2048],
                            accum_op=ALU.add).then_inc(S[f"outs{ci % 4}"], 16)

        g.memset(ones_f[:, :], 1.0)
        g.memset(s_state[:, :], 0.0)
        g.memset(acs4[:, :], 0.0)
        g.memset(rfold[:, :], 0.0)
        g.memset(stil[:, :], 0)
        g.memset(epsb[:, :], 1e-5).then_inc(S["init0"], 1)

        for tau in range(NT):
            for ws in range(24):
                src, row0 = ((wk_h, ws) if ws < 4 else
                             (wv_h, ws - 4) if ws < 8 else (wq_h, ws - 8))
                wsg = tau * 24 + ws
                if wsg >= 2:
                    g.wait_ge(S["mmp"], (wsg - 2) * 2 + 2)
                if tau == 1 and ws < 2:
                    g.wait_ge(S["mmgu"], (42 + ws) * 4 + 4)
                for hh in range(2):
                    g.dma_start(
                        out=wun[:, (ws % 2) * 2048 + hh * 1024:(ws % 2) * 2048 + hh * 1024 + 1024],
                        in_=src[row0, :, hh * 1024:hh * 1024 + 1024],
                    ).then_inc(P("wldp", wsg), 16)
            for part in range(NPART):
                for icH in range(NICP):
                    gl = part * NICP + icH
                    icg = tau * 44 + gl
                    if gl < 2:
                        if tau == 0 and gl == 0:
                            g.wait_ge(S["mmp"], 48)
                        if tau == 1:
                            g.wait_ge(S["mmp"], 96)
                            g.wait_ge(S["mmgu"], (42 + gl) * 4 + 4)
                    else:
                        g.wait_ge(S["mmgu"], tau * 176 + (gl - 2) * 4 + 4)
                    g.dma_start(
                        out=wun[:, (gl % 2) * 4096:(gl % 2) * 4096 + 4096],
                        in_=wgu_h[part * NICP + icH, :, :],
                    ).then_inc(P("wldgu", icg), 16)
                    if tau == 1 and part >= 2:
                        emit_fstep(NPRE)
                for dmc in range(16):
                    dsl = tau * 64 + part * 16 + dmc
                    if dsl >= 2:
                        pd = dsl - 2
                        ptau, prest = pd // 64, pd % 64
                        ppart, pdmc = prest // 16, prest % 16
                        g.wait_ge(S["mmdn"],
                                  ptau * 128 + ppart * 32 + pdmc * 2 + 2)
                    g.dma_start(
                        out=wdbuf[:, (dmc % 2) * 1408:(dmc % 2) * 1408 + 1408],
                        in_=bass.AP(wd_h, part * NICP * 128 * 2048 + dmc * 128,
                                    [[2048, 128], [128 * 2048, NICP], [1, 128]]),
                    ).then_inc(P("wldd", dsl), 16)
                    if tau == 1 and part >= 2:
                        emit_fstep(NPRE)
                if tau == 1 and part == 1:
                    g.wait_ge(S["ccw"], 64)
                    g.collective_compute(
                        "AllGather", ALU.bypass,
                        replica_groups=[list(range(NCORE))],
                        ins=[cc_in.ap().opt()],
                        outs=[cc_out.ap().opt()],
                    ).then_inc(S["cc"], 1)

        while FPOS[0] < len(FSTEPS):
            emit_fstep()

    @blk.sync
    def _(s):
        for src, dstb in [(bqt_h, bqt), (bkt_h, bkt), (bvt_h, bvt),
                          (gst_h, gst), (ust_h, ust), (dst_h, dst),
                          (idn_h, ident), (msk_h, mask8), (cm_h, cmb),
                          (cm1_h, cm1b)]:
            s.dma_start(out=dstb[:, :], in_=src[:, :]).then_inc(S["ini"], 16)

        for tau in range(NT):
            for sub in range(NS):
                ts = tau * NS + sub
                if ts >= 2:
                    s.wait_ge(S["xn"], ts - 1)
                    s.wait_ge(S["lna"], ts - 1)
                if tau >= 1 and sub == 0:
                    s.wait_ge(S["mmdn"], 128 * tau)
                for hh in range(2):
                    s.dma_start(out=xbuf[:, (ts % 2) * 2048 + hh * 1024:(ts % 2) * 2048 + hh * 1024 + 1024],
                                in_=x_h[ts * 128:(ts + 1) * 128, hh * 1024:hh * 1024 + 1024]
                                ).then_inc(P("xld", ts), 16)

            for it in PLANS[tau]:
                kind = it[0]
                if kind == "attn":
                    u = it[1]
                    ug = tau * NU + u
                    ci = tau * NCH + u // 4
                    gi = u % 4
                    s.wait_ge(S["qsd"], ug * 4 + 1)
                    s.dma_start(out=qh_scr[ci, gi, :, :],
                                in_=qside[:, 0:512]).then_inc(P("qhsp", ug), 16)
                    s.wait_ge(S["oev"], ug + 1)
                    s.dma_start(out=bass.AP(o_scr, ci * 128 * 2048 + gi * 512,
                                            [[2048, 128], [1, 512]]),
                                in_=ostg[:, 0:512]).then_inc(P("osp", ug), 16)
                elif kind == "dn":
                    part, ib = it[1], it[2]
                    dmc, j = ib // 2, ib % 2
                    dnn = part * 32 + ib
                    if tau == 1 and part == 2 and ib < 8:
                        if ib == 0:
                            s.wait_ge(S["cc"], 1)
                            s.wait_ge(S["kfs"], 16)
                            s.wait_ge(S["aprep"], 64)
                            s.wait_ge(S["aring"], 64)
                        for r in range(ib * 4, ib * 4 + 4):
                            s.dma_start(out=ccst[:, r * 129:(r + 1) * 129],
                                        in_=cc_out[r, :, :]).then_inc(S["ccl"], 16)
                    ev_i = tau * 128 + dnn
                    if part < 3:
                        spidx = tau * 96 + dnn
                        s.wait_ge(S["dnev"], ev_i + 1)
                        s.dma_start(
                            out=bass.AP(dnp_scr, dmc * 128 * 2048 + tau * 1024 + j * 512,
                                        [[2048, 128], [1, 512]]),
                            in_=dnstg[:, (ev_i % 2) * 512:(ev_i % 2) * 512 + 512],
                        ).then_inc(P("dnpsp", spidx), 16)
                    rid_l = dnn - 30
                    if 0 <= rid_l < 96:
                        rid = tau * 96 + rid_l
                        rdmc, rj = (rid_l % 32) // 2, rid_l % 2
                        if rid_l >= 2:
                            s.wait_ge(S["dnev"], tau * 128 + rid_l + 31)
                        s.wait_ge(P("dnpsp", rid), 16 * (rid // 2 + 1))
                        s.dma_start(
                            out=dnpld[:, (rid % 2) * 512:(rid % 2) * 512 + 512],
                            in_=bass.AP(dnp_scr, rdmc * 128 * 2048 + tau * 1024 + rj * 512,
                                        [[2048, 128], [1, 512]]),
                        ).then_inc(P("dnpld", rid), 16)
                elif kind == "tr":
                    i = it[1]
                    tq = tau * 32 + i
                    dmc, j = i // 2, i % 2
                    s.wait_ge(S["trqe"], tq + 1)
                    s.dma_start(
                        out=bass.AP(dn_scr,
                                    (tau * 8 + j * 4) * 128 * 2048 + dmc * 128,
                                    [[2048, 128], [128 * 2048, 4], [1, 128]]),
                        in_=trstg[:, 0:512],
                    ).then_inc(P("dnsp", tq), 16)
                elif kind == "cc":
                    s.wait_ge(S["pack"], 1)
                    for gi in range(4):
                        s.dma_start(out=cc_in[gi, :, :],
                                    in_=spack[:, gi * 129:(gi + 1) * 129]
                                    ).then_inc(S["ccw"], 16)

        for pp in range(2):
            s.wait_ge(S[f"qhsp{pp}"], 16 * 32)
        for uq in range(64):
            if uq >= 4:
                s.wait_ge(S["fmm"], uq - 3)
            s.dma_start(out=qhld[:, (uq % 4) * 512:(uq % 4) * 512 + 512],
                        in_=qh_scr[uq // 4, uq % 4, :, :]
                        ).then_inc(S[f"fqh{uq % 4}"], 16)
        for ff in range(4):
            s.wait_ge(S[f"outs{ff}"], 16 * 4)

    @blk.tensor
    def _(t):
        t.wait_ge(S["ini"], 160)

        def emit_attn(u, tau):
            ug = tau * NU + u
            ci = u // 4
            gi = u % 4
            slot = ug % NRING
            t.wait_ge(S["aring"], ug)
            t.wait_ge(S["aprep"], ug + 1)
            t.wait_ge(S["trev"], (tau + 1) * 160)
            for j in range(4):
                t.wait_ge(S["qsd"], ug * 4 + j + 1)
                t.matmul(ps_sc[32 * j:32 * (j + 1), :],
                         kring[:, slot * 128 + 32 * j:slot * 128 + 32 * j + 32],
                         qside[:, j * 512:(j + 1) * 512],
                         start=True, stop=True,
                         tile_position=(0, 32 * j)).then_inc(S["scj"], 1)
            t.wait_ge(S["stl"], ug + 1)
            t.transpose(ps_sc[:, 0:128], khring[:, slot * 128:slot * 128 + 128],
                        ident[:, :]).then_inc(S["trk"], 1)
            t.wait_ge(S["khe"], ug + 1)
            t.matmul(ps_sc[:, 0:128],
                     khtb[:, (ug % 2) * 128:(ug % 2) * 128 + 128],
                     vtok[:, ci * 512 + gi * 128:ci * 512 + gi * 128 + 128],
                     start=True, stop=True).then_inc(S["umm"], 1)
            t.wait_ge(S["ssc"], ug + 1)
            if ug >= 2:
                t.wait_ge(S["oev"], ug - 1)
            last = None
            for hh in range(4):
                t.matmul(ps_o[ug % 2][:, hh * 128:(hh + 1) * 128],
                         stil[:, (ug % 2) * 512 + hh * 128:(ug % 2) * 512 + hh * 128 + 128],
                         vtok[:, ci * 512 + gi * 128:ci * 512 + gi * 128 + 128],
                         start=True, stop=False)
                last = t.matmul(ps_o[ug % 2][:, hh * 128:(hh + 1) * 128],
                                qside[:, hh * 128:hh * 128 + 128],
                                sscal[:, (ug % 2) * 128:(ug % 2) * 128 + 128],
                                start=False, stop=True)
            last.then_inc(S["omm"], 1)

        for tau in range(NT):
            for sub in range(NS):
                ts = tau * NS + sub
                t.wait_ge(S["xn"], ts + 1)
                for k in range(16):
                    trn = tau * 160 + sub * 16 + k
                    if trn >= 3:
                        t.wait_ge(S["trev"], trn - 2)
                    if tau >= 1 and sub == 0 and k == 0:
                        t.wait_ge(S["oev"], tau * NU)
                        t.wait_ge(S["aring"], tau * NU)
                    t.transpose(trb[trn % 3][:, 0:128],
                                xn_tok[:, k * 128:(k + 1) * 128],
                                ident[:, :]).then_inc(S["trp"], 1)
            for pg, (fam, oc, j) in enumerate(PROJ):
                gidx = gid(("proj", tau, pg))
                ws = pslab_of(fam, oc)
                wsg = tau * 24 + ws
                t.wait_ge(P("wldp", wsg), 32 * (wsg // 2 + 1))
                t.wait_ge(S["trev"], tau * 160 + 64 * (j + 1))
                bank_wait(t, gidx)
                bank = ps_mm[gidx % 4]
                last = None
                for kk in range(16):
                    last = t.matmul(
                        bank[:, :],
                        wun[:, (ws % 2) * 2048 + kk * 128:(ws % 2) * 2048 + kk * 128 + 128],
                        xnT[:, kk * 1024 + j * 512:kk * 1024 + j * 512 + 512],
                        start=(kk == 0), stop=(kk == 15))
                last.then_inc(S["mmp"], 1)
                if fam == "v":
                    vg = oc * 2 + j
                    for b in range(4):
                        trn = tau * 160 + 128 + vg * 4 + b
                        t.wait_ge(S["vfms"], tau * 8 + vg + 1)
                        if trn >= 3:
                            t.wait_ge(S["trev"], trn - 2)
                        t.transpose(trb[trn % 3][:, 0:128],
                                    vstg[:, (vg % 2) * 512 + b * 128:(vg % 2) * 512 + b * 128 + 128],
                                    ident[:, :]).then_inc(S["trp"], 1)
            for it in PLANS[tau]:
                kind = it[0]
                if kind == "attn":
                    emit_attn(it[1], tau)
                elif kind == "gu":
                    part, icH, sub = it[1], it[2], it[3]
                    gl = part * NICP + icH
                    icg = tau * 44 + gl
                    gu = sub % 2
                    j = sub // 2
                    if sub == 0:
                        t.wait_ge(P("wldgu", icg), 16 * (icg // 2 + 1))
                    gidx = gid((tau,) + it)
                    bank_wait(t, gidx)
                    bank = ps_mm[gidx % 4]
                    last = None
                    for kk in range(16):
                        last = t.matmul(
                            bank[:, :],
                            wun[:, (gl % 2) * 4096 + gu * 2048 + kk * 128:
                                 (gl % 2) * 4096 + gu * 2048 + kk * 128 + 128],
                            xnT[:, kk * 1024 + j * 512:kk * 1024 + j * 512 + 512],
                            start=(kk == 0), stop=(kk == 15))
                    last.then_inc(S["mmgu"], 1)
                elif kind == "dn":
                    part, ib = it[1], it[2]
                    dmc, j = ib // 2, ib % 2
                    dsl = tau * 64 + part * 16 + dmc
                    t.wait_ge(P("wldd", dsl), 16 * (dsl // 2 + 1))
                    t.wait_ge(S["hcmb"], tau * 88 + part * 22 + 21 + j)
                    gidx = gid((tau,) + it)
                    bank_wait(t, gidx)
                    bank = ps_mm[gidx % 4]
                    last = None
                    for icH in range(NICP):
                        last = t.matmul(
                            bank[:, :],
                            wdbuf[:, (dmc % 2) * 1408 + icH * 128:(dmc % 2) * 1408 + icH * 128 + 128],
                            hbuf[:, icH * 1024 + j * 512:icH * 1024 + j * 512 + 512],
                            start=(icH == 0), stop=(icH == NICP - 1))
                    last.then_inc(S["mmdn"], 1)
                elif kind == "tr":
                    i = it[1]
                    ev_i = tau * 128 + 96 + i
                    t.wait_ge(S["dnev"], ev_i + 1)
                    gidx = gid((tau,) + it)
                    bank_wait(t, gidx)
                    bank = ps_mm[gidx % 4]
                    last = None
                    for b in range(4):
                        last = t.transpose(
                            bank[:, b * 128:(b + 1) * 128],
                            dnstg[:, (ev_i % 2) * 512 + b * 128:(ev_i % 2) * 512 + b * 128 + 128],
                            ident[:, :])
                    last.then_inc(S["trq"], 1)

        for uq in range(64):
            t.wait_ge(S[f"fqh{uq % 4}"], 16 * (uq // 4 + 1))
            t.wait_ge(S["scorr"], uq + 1)
            gidx = gid((1, "corr", uq))
            bank_wait(t, gidx)
            bank = ps_mm[gidx % 4]
            last = None
            for hh in range(4):
                last = t.matmul(bank[:, hh * 128:(hh + 1) * 128],
                                qhld[:, (uq % 4) * 512 + hh * 128:(uq % 4) * 512 + hh * 128 + 128],
                                scorb[:, (uq % 4) * 128:(uq % 4) * 128 + 128],
                                start=True, stop=True)
            last.then_inc(S["fmm"], 1)

    @blk.scalar
    def _(a):
        a.wait_ge(S["ini"], 160)
        a.wait_ge(S["prep"], 1)

        def emit_burst(tau, c):
            if c + 1 < NCH:
                c2 = c + 1
                a.wait_ge(S["kfs"], tau * 8 + 7 + c2 // 4)
                a.wait_ge(S["aarg"], tau * NU + c2 * 4)
                a.activation(out=gbuf[:, :],
                             in_=bass.AP(kf, c2 * 128, [[4096, 128], [1024, 4], [1, 128]]),
                             func=AF.Ln, scale=-1.0,
                             bias=ones_f[:, 0:1]).then_inc(S["agn"], 1)
            C = tau * NCH + c
            bb = (C % 3) * 512
            for gg in range(4):
                ug = tau * NU + c * 4 + gg
                a.wait_ge(S["aarg"], ug + 1)
                if ug >= 4:
                    a.wait_ge(S["aprep"], ug - 3)
                if ug >= NRING:
                    a.wait_ge(S["aring"], ug - NRING + 1)
                slot = ug % NRING
                ks4 = (ug % 4) * 128
                bsl = bbuf3[:, bb + gg * 128:bb + gg * 128 + 128]
                for j in range(4):
                    a.activation(out=ering[:, slot * 512 + j * 128:slot * 512 + j * 128 + 128],
                                 in_=bsl, func=AF.Exp,
                                 bias=nb4r[:, slot * 4 + j:slot * 4 + j + 1])
                for j in range(4):
                    a.activation(out=kexpb[:, ks4 + 32 * j:ks4 + 32 * j + 32],
                                 in_=bbuf3[:, bb + gg * 128 + 32 * j:bb + gg * 128 + 32 * j + 32],
                                 func=AF.Exp, scale=-1.0,
                                 bias=bbuf3[:, bb + gg * 128 + 32 * j + 15:bb + gg * 128 + 32 * j + 16])
                a.activation(out=kh2eb[:, ks4:ks4 + 128], in_=bsl, func=AF.Exp,
                             scale=-1.0,
                             bias=bbuf3[:, bb + gg * 128 + 127:bb + gg * 128 + 128])
                a.activation(out=ebring[:, slot * 2:slot * 2 + 2],
                             in_=bass.AP(bbuf3, bb + gg * 128 + 15, [[3 * 512, 128], [112, 2]]),
                             func=AF.Exp).then_inc(S["aexp"], 1)

        for tau in range(NT):
            for sub in range(NS + 1):
                if sub >= 1:
                    psub = sub - 1
                    for k in range(16):
                        trn = tau * 160 + psub * 16 + k
                        a.wait_ge(S["trp"], trn + 1)
                        if tau >= 1 and psub == 0 and k == 0:
                            a.wait_ge(S["mmgu"], tau * 176)
                        a.activation(
                            out=xnT[:, k * 1024 + psub * 128:k * 1024 + psub * 128 + 128],
                            in_=trb[trn % 3][:, 0:128],
                            func=AF.Copy).then_inc(S["trev"], 1)
                if sub >= NS:
                    continue
                ts = tau * NS + sub
                a.wait_ge(S["var"], ts + 1)
                AV[0] += 1
                a.activation(out=lnv[:, ts:ts + 1], in_=varb[:, ts:ts + 1],
                             func=AF.Ln, bias=epsb[:, 0:1]).then_inc(S["av"], 1)
                a.wait_ge(S["av"], AV[0])
                a.activation(out=rstd[:, ts:ts + 1], in_=lnv[:, ts:ts + 1],
                             func=AF.Exp, scale=-0.5).then_inc(S["rstd"], 1)
                a.wait_ge(S["nmr"], ts + 1)
                a.wait_ge(S["rstd"], ts + 1)
                if ts >= 1:
                    a.wait_ge(S["trp"], tau * 160 + sub * 16)
                a.activation(out=xn_tok[:, :],
                             in_=xbuf[:, (ts % 2) * 2048:(ts % 2) * 2048 + 2048],
                             func=AF.Identity, scale=rstd[:, ts:ts + 1],
                             bias=nmr[:, ts:ts + 1]).then_inc(S["xn"], 1)
            for pg, (fam, oc, j) in enumerate(PROJ):
                gidx = gid(("proj", tau, pg))
                bank = ps_mm[gidx % 4]
                a.wait_ge(S["mmp"], tau * 48 + pg + 1)
                if fam == "k":
                    kp = tau * 8 + oc * 2 + j
                    if kp >= 2:
                        a.wait_ge(S["kfs"], kp - 1)
                    a.activation(out=ekb[:, (kp % 2) * 512:(kp % 2) * 512 + 512],
                                 in_=bank[:, :], func=AF.Exp,
                                 scale=-1.0, bias=nbkt[:, oc:oc + 1]
                                 ).then_inc(S["ek"], 1)
                elif fam == "v":
                    vg = oc * 2 + j
                    if vg >= 2:
                        a.wait_ge(S["trp"], tau * 160 + 128 + 4 * vg - 4)
                    elif tau >= 1:
                        a.wait_ge(S["trp"], (tau - 1) * 160 + 128 + 4 * (vg + 6) + 4)
                    a.activation(out=vstg[:, (vg % 2) * 512:(vg % 2) * 512 + 512],
                                 in_=bank[:, :], func=AF.Identity,
                                 bias=bvt[:, oc:oc + 1]).then_inc(S["vfms"], 1)
                else:
                    if tau >= 1 and oc == 0 and j == 0:
                        a.wait_ge(S["qsd"], tau * 128)
                    a.activation(out=qs[:, oc * 1024 + j * 512:oc * 1024 + j * 512 + 512],
                                 in_=bank[:, :], func=AF.Relu, scale=SCALE,
                                 bias=bqs[:, oc:oc + 1]).then_inc(S["qss"], 1)
            a.wait_ge(S["kfs"], tau * 8 + 7)
            if tau >= 1:
                a.wait_ge(S["aarg"], tau * NU)
            a.activation(out=gbuf[:, :],
                         in_=bass.AP(kf, 0, [[4096, 128], [1024, 4], [1, 128]]),
                         func=AF.Ln, scale=-1.0,
                         bias=ones_f[:, 0:1]).then_inc(S["agn"], 1)
            emit_burst(tau, 0)
            for it in PLANS[tau]:
                kind = it[0]
                if kind == "attn" and it[1] % 4 == 0 and it[1] >= 4:
                    emit_burst(tau, it[1] // 4)
                elif kind == "cc":
                    a.wait_ge(S["aarg"], 2 * NU)
                    a.activation(out=dtot4[:, :], in_=acs4[:, :],
                                 func=AF.Exp).then_inc(S["dtot"], 1)
                    a.activation(out=eab[:, :], in_=bmacs[:, :],
                                 func=AF.Exp).then_inc(S["ea"], 1)
                elif kind == "gu" and it[3] % 2 == 0:
                    part, icH, sub = it[1], it[2], it[3]
                    j = sub // 2
                    ic = part * NICP + icH
                    hc = tau * 88 + part * 22 + icH * 2 + j
                    a.wait_ge(S["mmgu"], tau * 176 + ic * 4 + sub + 1)
                    if hc >= 2:
                        a.wait_ge(S["hcmb"], hc - 1)
                    a.activation(out=sgb[:, (j % 2) * 512:(j % 2) * 512 + 512],
                                 in_=ps_mm[gid((tau,) + it) % 4][:, :], func=AF.Silu,
                                 scale=gst[:, ic:ic + 1]).then_inc(S["silu"], 1)

    @blk.vector
    def _(v):
        v.wait_ge(S["ini"], 160)
        v.wait_ge(S["init0"], 1)
        v.tensor_scalar_mul(bqs[:, :], bqt[:, :], SCALE)
        v.tensor_scalar_mul(nbkt[:, :], bkt[:, :], -1.0).then_inc(S["prep"], 1)

        def emit_scan(c, tau):
            C = tau * NCH + c
            bb = (C % 3) * 512
            v.wait_ge(S["agn"], C + 1)
            if C >= 3:
                v.wait_ge(S["aexp"], 4 * C - 8)
            VV[0] += 1
            for g2 in range(4):
                last_scan = v.tensor_tensor_scan(
                    out=bbuf3[:, bb + g2 * 128:bb + (g2 + 1) * 128],
                    data0=ones_f[:, :],
                    data1=gbuf[:, g2 * 128:(g2 + 1) * 128],
                    initial=0.0, op0=ALU.mult, op1=ALU.add)
            last_scan.then_inc(S["vv"], 1)
            v.wait_ge(S["vv"], VV[0])

        def emit_args(u, tau):
            ug = tau * NU + u
            c = u // 4
            gg = u % 4
            C = tau * NCH + c
            bb = (C % 3) * 512
            if ug >= NRING:
                v.wait_ge(S["aexp"], ug - NRING + 1)
            if ug >= 4:
                v.wait_ge(S["aarg"], ug - 3)
            slot = ug % NRING
            v.tensor_scalar_mul(nb4r[:, slot * 4:slot * 4 + 4],
                                bass.AP(bbuf3, bb + gg * 128 + 15, [[3 * 512, 128], [32, 4]]),
                                -1.0)
            VV[0] += 1
            v.tensor_add(bmacs[:, ug:ug + 1], acs4[:, gg:gg + 1],
                         bbuf3[:, bb + gg * 128 + 15:bb + gg * 128 + 16]).then_inc(S["vv"], 1)
            v.wait_ge(S["vv"], VV[0])
            v.tensor_add(acs4[:, gg:gg + 1], acs4[:, gg:gg + 1],
                         bbuf3[:, bb + gg * 128 + 127:bb + gg * 128 + 128]).then_inc(S["aarg"], 1)

        def emit_attn_v(u, tau):
            ug = tau * NU + u
            c = u // 4
            gg = u % 4
            slot = ug % NRING
            if ug >= 4:
                v.wait_ge(S["aring"], ug - 3)
            v.wait_ge(S["aexp"], ug + 1)
            ks4 = (ug % 4) * 128
            kfsl = kf[:, gg * 1024 + c * 128:gg * 1024 + c * 128 + 128]
            v.tensor_mul(kring[:, slot * 128:(slot + 1) * 128], kfsl,
                         kexpb[:, ks4:ks4 + 128])
            v.tensor_mul(khring[:, slot * 128:(slot + 1) * 128], kfsl,
                         kh2eb[:, ks4:ks4 + 128]).then_inc(S["aprep"], 1)
            qsrc = bass.AP(qs, gg * 4 * 1024 + c * 128, [[16384, 128], [1024, 4], [1, 128]])
            v.wait_ge(S["qss"], tau * 32 + 8 * gg + 7 + c // 4)
            for j in range(4):
                if ug >= 1:
                    v.wait_ge(S["scj"], (ug - 1) * 4 + j + 1)
                    if j == 0:
                        v.wait_ge(S["omm"], ug)
                        v.wait_ge(P("qhsp", ug - 1), 16 * ((ug - 1) // 2 + 1))
                v.tensor_mul(
                    qside[:, j * 512:(j + 1) * 512],
                    qsrc,
                    bass.AP(ering, slot * 512 + j * 128, [[NRING * 512, 128], [0, 4], [1, 128]]),
                ).then_inc(S["qsd"], 1)
            v.tensor_scalar_mul(sscal[:, (ug % 2) * 128:(ug % 2) * 128 + 128],
                                s_state[:, gg * 128:(gg + 1) * 128],
                                ebring[:, slot * 2:slot * 2 + 1]).then_inc(S["ssc"], 1)
            v.wait_ge(S["scj"], ug * 4 + 4)
            v.copy_predicated(out=stil[:, (ug % 2) * 512:(ug % 2) * 512 + 512],
                              mask=mask8[:, :],
                              data=ps_sc[:, :]).then_inc(S["stl"], 1)
            v.wait_ge(S["trk"], ug + 1)
            v.tensor_copy(khtb[:, (ug % 2) * 128:(ug % 2) * 128 + 128],
                          ps_sc[:, 0:128]).then_inc(S["khe"], 1)
            v.wait_ge(S["umm"], ug + 1)
            v.scalar_tensor_tensor(
                out=s_state[:, gg * 128:(gg + 1) * 128],
                in0=s_state[:, gg * 128:(gg + 1) * 128],
                scalar=ebring[:, slot * 2 + 1:slot * 2 + 2],
                in1=ps_sc[:, 0:128],
                op0=ALU.mult, op1=ALU.add).then_inc(S["aring"], 1)
            v.wait_ge(S["omm"], ug + 1)
            if ug >= 1:
                v.wait_ge(P("osp", ug - 1), 16 * ((ug - 1) // 2 + 1))
            v.tensor_copy(ostg[:, 0:512],
                          ps_o[ug % 2][:, :]).then_inc(S["oev"], 1)
            w = u + 5
            if w < NU:
                if w % 4 == 0:
                    emit_scan(w // 4, tau)
                emit_args(w, tau)

        def emit_fold():
            v.wait_ge(S["ccl"], 16 * 32)
            emit_fold_body()

        def emit_fold_body():
            for jj in range(8):
                    ii = None
                    for gg in range(4):
                        r = jj * 4 + gg
                        ii = v.tensor_scalar(dpcol4[:, gg:gg + 1],
                                             ccst[:, r * 129:r * 129 + 1],
                                             cmb[:, jj:jj + 1],
                                             cm1b[:, jj:jj + 1], ALU.mult, ALU.add)
                    VV[0] += 1
                    ii.then_inc(S["vv"], 1)
                    v.wait_ge(S["vv"], VV[0])
                    for gg in range(4):
                        ii = v.tensor_scalar(rtmp4[:, gg * 129:(gg + 1) * 129],
                                             rfold[:, gg * 129:(gg + 1) * 129],
                                             dpcol4[:, gg:gg + 1], None, ALU.mult)
                    VV[0] += 1
                    ii.then_inc(S["vv"], 1)
                    v.wait_ge(S["vv"], VV[0])
                    for gg in range(4):
                        r = jj * 4 + gg
                        ii = v.scalar_tensor_tensor(
                            out=rfold[:, gg * 129:(gg + 1) * 129],
                            in0=ccst[:, r * 129:(r + 1) * 129],
                            scalar=cmb[:, jj:jj + 1],
                            in1=rtmp4[:, gg * 129:(gg + 1) * 129],
                            op0=ALU.mult, op1=ALU.add)
                    VV[0] += 1
                    ii.then_inc(S["vv"], 1)
                    v.wait_ge(S["vv"], VV[0])
            v.memset(dpcol4[:, 0:1], 0.0).then_inc(S["fold"], 1)

        for tau in range(NT):
            for sub in range(NS):
                if True:
                    ts = tau * NS + sub
                    v.wait_ge(P("xld", ts), 32 * (ts // 2 + 1))
                    xb = xbuf[:, (ts % 2) * 2048:(ts % 2) * 2048 + 2048]
                    v.reduce_sum(stat1[:, ts:ts + 1], xb, axis=AX.X)
                    if tau >= 1 and sub == 0:
                        v.wait_ge(S["mmdn"], 128 * tau)
                    v.scalar_tensor_tensor(
                        out=sqscr[:, :], in0=xb, scalar=1.0, in1=xb,
                        op0=ALU.mult, op1=ALU.mult,
                        accum_out=stat2[:, ts:ts + 1]).then_inc(S["lna"], 1)
                    v.wait_ge(S["lna"], ts + 1)
                    v.tensor_scalar_mul(meanv[:, ts:ts + 1], stat1[:, ts:ts + 1], 1.0 / 2048)
                    VV[0] += 1
                    v.tensor_scalar_mul(em2[:, ts:ts + 1], stat2[:, ts:ts + 1], 1.0 / 2048).then_inc(S["vv"], 1)
                    v.wait_ge(S["vv"], VV[0])
                    VV[0] += 1
                    v.tensor_mul(varb[:, ts:ts + 1], meanv[:, ts:ts + 1], meanv[:, ts:ts + 1]).then_inc(S["vv"], 1)
                    v.wait_ge(S["vv"], VV[0])
                    v.tensor_sub(varb[:, ts:ts + 1], em2[:, ts:ts + 1], varb[:, ts:ts + 1]).then_inc(S["var"], 1)
                    v.wait_ge(S["rstd"], ts + 1)
                    v.scalar_tensor_tensor(
                        out=nmr[:, ts:ts + 1], in0=meanv[:, ts:ts + 1], scalar=-1.0,
                        in1=rstd[:, ts:ts + 1], op0=ALU.mult, op1=ALU.mult).then_inc(S["nmr"], 1)
            for oc in range(4):
                for j in range(2):
                    kp = tau * 8 + oc * 2 + j
                    v.wait_ge(S["ek"], kp + 1)
                    if tau >= 1 and oc == 0 and j == 0:
                        v.wait_ge(S["aprep"], tau * NU)
                    if kp >= 1:
                        v.wait_ge(S["kfs"], kp)
                    VV[0] += 1
                    v.tensor_scalar_add(kpost[:, :],
                                        ekb[:, (kp % 2) * 512:(kp % 2) * 512 + 512], 1.0).then_inc(S["vv"], 1)
                    v.wait_ge(S["vv"], VV[0])
                    VV[0] += 1
                    v.reciprocal(kpost2[:, :], kpost[:, :]).then_inc(S["vv"], 1)
                    v.wait_ge(S["vv"], VV[0])
                    v.tensor_scalar_min(
                        kf[:, oc * 1024 + j * 512:oc * 1024 + j * 512 + 512],
                        kpost2[:, :], 0.95).then_inc(S["kfs"], 1)
            for voc in range(4):
                for j in range(2):
                    for b in range(4):
                        vg = voc * 2 + j
                        trn = tau * 160 + 128 + vg * 4 + b
                        vs = j * 4 + b
                        v.wait_ge(S["trp"], trn + 1)
                        if tau >= 1 and vg == 0 and b == 0:
                            v.wait_ge(S["omm"], tau * NU)
                            v.wait_ge(S["umm"], tau * NU)
                        v.tensor_copy(
                            vtok[:, vs * 512 + voc * 128:vs * 512 + voc * 128 + 128],
                            trb[trn % 3][:, 0:128]).then_inc(S["trev"], 1)
            emit_scan(0, tau)
            for u0 in range(4):
                emit_args(u0, tau)
            emit_scan(1, tau)
            emit_args(4, tau)
            for it in PLANS[tau]:
                kind = it[0]
                if kind == "attn":
                    emit_attn_v(it[1], tau)
                elif kind == "gu" and it[3] % 2 == 1:
                    part, icH, sub = it[1], it[2], it[3]
                    j = sub // 2
                    ic = part * NICP + icH
                    hc = tau * 88 + part * 22 + icH * 2 + j
                    v.wait_ge(S["mmgu"], tau * 176 + ic * 4 + sub + 1)
                    v.wait_ge(S["silu"], hc + 1)
                    if part == 0 and icH == 0 and j == 0:
                        v.wait_ge(S["trp"], tau * 160 + 128)
                        v.wait_ge(S["xn"], 8 * (tau + 1))
                        v.wait_ge(S["lna"], 8 * (tau + 1))
                        v.wait_ge(S["mmdn"], 128 * tau)
                    v.scalar_tensor_tensor(
                        out=hbuf[:, icH * 1024 + j * 512:icH * 1024 + j * 512 + 512],
                        in0=ps_mm[gid((tau,) + it) % 4][:, :],
                        scalar=ust[:, ic:ic + 1],
                        in1=sgb[:, (j % 2) * 512:(j % 2) * 512 + 512],
                        op0=ALU.mult, op1=ALU.mult).then_inc(S["hcmb"], 1)
                elif kind == "dn":
                    part, ib = it[1], it[2]
                    dmc, j = ib // 2, ib % 2
                    ev_i = tau * 128 + part * 32 + ib
                    evg = part * 32 + ib
                    if tau == 1 and part == 3 and ib == 0:
                        emit_fold()
                    v.wait_ge(S["mmdn"], ev_i + 1)
                    dnstg_free_wait(v, tau, evg)
                    if part == 0:
                        v.tensor_scalar_mul(
                            dnstg[:, (ev_i % 2) * 512:(ev_i % 2) * 512 + 512],
                            ps_mm[gid((tau,) + it) % 4][:, :],
                            dst[:, dmc:dmc + 1]).then_inc(S["dnev"], 1)
                    else:
                        rid = tau * 96 + (part - 1) * 32 + ib
                        v.wait_ge(P("dnpld", rid), 16 * (rid // 2 + 1))
                        v.scalar_tensor_tensor(
                            out=dnstg[:, (ev_i % 2) * 512:(ev_i % 2) * 512 + 512],
                            in0=ps_mm[gid((tau,) + it) % 4][:, :],
                            scalar=dst[:, dmc:dmc + 1],
                            in1=dnpld[:, (rid % 2) * 512:(rid % 2) * 512 + 512],
                            op0=ALU.mult, op1=ALU.add).then_inc(S["dnev"], 1)
                elif kind == "tr":
                    i = it[1]
                    tq = tau * 32 + i
                    v.wait_ge(S["trq"], tq + 1)
                    if tq >= 1:
                        v.wait_ge(P("dnsp", tq - 1), 16 * ((tq - 1) // 2 + 1))
                    trg = gid((tau, "tr", i))
                    v.tensor_copy(trstg[:, 0:512],
                                  ps_mm[trg % 4][:, :]).then_inc(S["trqe"], 1)
                elif kind == "cc":
                    v.wait_ge(S["aring"], 2 * NU)
                    v.wait_ge(S["dtot"], 1)
                    pk = None
                    for gg in range(4):
                        v.tensor_copy(spack[:, gg * 129:gg * 129 + 1], dtot4[:, gg:gg + 1])
                        pk = v.tensor_copy(spack[:, gg * 129 + 1:(gg + 1) * 129],
                                           s_state[:, gg * 128:(gg + 1) * 128])
                    pk.then_inc(S["pack"], 1)

        v.wait_ge(S["ea"], 1)

        def emit_cevac(cq):
            v.wait_ge(S["fmm"], cq + 1)
            if cq >= 8:
                oc = cq // 4 - 2
                v.wait_ge(S[f"outs{oc % 4}"], 16 * (oc // 4 + 1))
            v.tensor_copy(cstg[:, (cq % 8) * 512:(cq % 8) * 512 + 512],
                          ps_mm[gid((1, "corr", cq)) % 4][:, :]
                          ).then_inc(S["cev"], 1)

        def emit_scorr(uq):
            v.tensor_scalar_mul(
                scorb[:, (uq % 4) * 128:(uq % 4) * 128 + 128],
                rfold[:, (uq % 4) * 129 + 1:(uq % 4) * 129 + 129],
                eab[:, uq:uq + 1]).then_inc(S["scorr"], 1)

        for uq in range(4):
            emit_scorr(uq)
        for uq in range(64):
            emit_cevac(uq)
            if uq + 4 < 64:
                emit_scorr(uq + 4)

    bg.__exit__(None, None, None)
    return nc


_NC_CACHE = {}


def _get_nc():
    if "nc" not in _NC_CACHE:
        _NC_CACHE["nc"] = build_nc()
    return _NC_CACHE["nc"]


def _prep_inputs(kw):
    f = np.float32
    x = np.asarray(kw["L0_attn_hidden_states"], f).reshape(16384, 2048)
    Wq = np.asarray(kw["L0_attn_Wq"], f)
    Wk = np.asarray(kw["L0_attn_Wk"], f)
    Wv = np.asarray(kw["L0_attn_Wv"], f)
    gw = np.asarray(kw["L0_ffn_gate_w"], f)
    uw = np.asarray(kw["L0_ffn_up_w"], f)
    dw = np.asarray(kw["L0_ffn_down_w"], f)

    def proj_slabs(W, noc):
        a = W.reshape(noc, 128, 16, 128).transpose(0, 3, 2, 1)
        return np.ascontiguousarray(a.reshape(noc, 128, 2048))

    wq_s = proj_slabs(Wq, 16)
    wk_s = proj_slabs(Wk, 4)
    wv_s = proj_slabs(Wv, 4)

    def ffn_slab(W):
        return W.reshape(44, 128, 16, 128).transpose(0, 3, 2, 1)

    gs_ = ffn_slab(gw)
    us_ = ffn_slab(uw)
    wgu_s = np.ascontiguousarray(
        np.stack([gs_, us_], axis=2).reshape(44, 128, 4096))
    wd_s = np.ascontiguousarray(dw.T.reshape(44, 128, 2048))

    def colmajor(vec, n):
        return np.ascontiguousarray(np.asarray(vec, f).reshape(n, 128).T)

    base = {
        "wq_s": wq_s, "wk_s": wk_s, "wv_s": wv_s, "wgu_s": wgu_s, "wd_s": wd_s,
        "bqt": colmajor(kw["L0_attn_bq"], 16),
        "bkt": colmajor(kw["L0_attn_bk"], 4),
        "bvt": colmajor(kw["L0_attn_bv"], 4),
        "gst": colmajor(kw["L0_ffn_gate_s"], 44),
        "ust": colmajor(kw["L0_ffn_up_s"], 44),
        "dst": colmajor(kw["L0_ffn_down_s"], 16),
        "identf": np.eye(128, dtype=f),
        "mask8": np.ascontiguousarray(
            (np.arange(128)[None, :] >= np.arange(128)[:, None])[:, None, :]
            .repeat(4, axis=1).reshape(128, 512).astype(np.int8)),
    }
    maps = []
    for core in range(NCORE):
        m = dict(base)
        m["x"] = np.ascontiguousarray(x[core * 2048:(core + 1) * 2048])
        cmv = np.zeros((128, 8), f)
        cmv[:, :core] = 1.0
        m["cmask"] = cmv
        cm1 = np.zeros((128, 8), f)
        cm1[:, core:] = 1.0
        m["cmask1"] = cm1
        maps.append(m)
    return maps


def kernel(**inputs):
    nc = _get_nc()
    in_maps = _prep_inputs(inputs)
    res = run_bass_kernel_spmd(nc, in_maps, core_ids=list(range(NCORE)))
    out = np.concatenate([res.results[i]["out"] for i in range(NCORE)], axis=0)
    return out.reshape(1, 16384, 2048).astype(np.float32)


# version 13
# speedup vs baseline: 1.0221x; 1.0064x over previous
import numpy as np
import ml_dtypes
import concourse.bass as bass
import concourse.mybir as mybir
import concourse.bass_utils as _bu
from concourse.bass_utils import run_bass_kernel_spmd

F32 = mybir.dt.float32
BF16 = mybir.dt.bfloat16
I8 = mybir.dt.int8
AF = mybir.ActivationFunctionType
ALU = mybir.AluOpType
AX = mybir.AxisListType

NCORE = 8
TCORE = 2048
NT = 2
TT = 1024
NS = 8
NCH = 8
NU = 32
NRING = 8
NPART = 4
NICP = 11
SCALE = 0.08838834764831845
DM = 2048


def build_plan(tau):
    items = []
    for part in range(NPART):
        for icH in range(NICP):
            for sub in range(4):
                items.append(("gu", part, icH, sub))
        if part < 3:
            for ib in range(32):
                items.append(("dn", part, ib))
        else:
            for i in range(32):
                items.append(("dn", part, i))
                if i >= 1:
                    items.append(("tr", i - 1))
            items.append(("tr", 31))
    merged = []
    ai = 0
    ci = 0
    for idx, it in enumerate(items):
        while ai < NU and idx >= int(1.5 * ai):
            merged.append(("attn", ai))
            ai += 1
        if tau == 1 and idx == 97:
            merged.append(("cc",))
        merged.append(it)
    while ai < NU:
        merged.append(("attn", ai))
        ai += 1
    return merged


PLANS = [build_plan(0), build_plan(1)]

GIDX = {}
_g = 0
for _tau in range(NT):
    for _it in PLANS[_tau]:
        if _it[0] in ("gu", "dn", "tr", "corr"):
            GIDX[(_tau,) + _it] = _g
            _g += 1
NGRP = _g


def build_nc():
    nc = bass.Bass("TRN2", target_bir_lowering=False)

    x_h = nc.dram_tensor("x", [TCORE, DM], F32, kind="ExternalInput")
    wq_h = nc.dram_tensor("wq_s", [16, 128, 2048], F32, kind="ExternalInput")
    wk_h = nc.dram_tensor("wk_s", [4, 128, 2048], F32, kind="ExternalInput")
    wv_h = nc.dram_tensor("wv_s", [4, 128, 2048], F32, kind="ExternalInput")
    wgu_h = nc.dram_tensor("wgu_s", [44, 128, 4096], F32, kind="ExternalInput")
    wd_h = nc.dram_tensor("wd_s", [44, 128, 2048], F32, kind="ExternalInput")
    bqt_h = nc.dram_tensor("bqt", [128, 16], F32, kind="ExternalInput")
    bkt_h = nc.dram_tensor("bkt", [128, 4], F32, kind="ExternalInput")
    bvt_h = nc.dram_tensor("bvt", [128, 4], F32, kind="ExternalInput")
    gst_h = nc.dram_tensor("gst", [128, 44], F32, kind="ExternalInput")
    ust_h = nc.dram_tensor("ust", [128, 44], F32, kind="ExternalInput")
    dst_h = nc.dram_tensor("dst", [128, 16], F32, kind="ExternalInput")
    idn_h = nc.dram_tensor("identf", [128, 128], F32, kind="ExternalInput")
    msk_h = nc.dram_tensor("mask8", [128, 512], I8, kind="ExternalInput")
    cm_h = nc.dram_tensor("cmask", [128, 8], F32, kind="ExternalInput")
    cm1_h = nc.dram_tensor("cmask1", [128, 8], F32, kind="ExternalInput")

    out_h = nc.dram_tensor("out", [TCORE, DM], F32, kind="ExternalOutput")

    o_scr = nc.dram_tensor("o_scr", [16, 128, 2048], F32, kind="Internal")
    qh_scr = nc.dram_tensor("qh_scr", [16, 4, 128, 512], BF16, kind="Internal")
    dnp_scr = nc.dram_tensor("dnp_scr", [16, 128, 2048], F32, kind="Internal")
    dn_scr = nc.dram_tensor("dn_scr", [16, 128, 2048], F32, kind="Internal")
    cc_in = nc.dram_tensor("cc_in", [4, 128, 129], F32, kind="Internal")
    cc_out = nc.dram_tensor("cc_out", [32, 128, 129], F32, kind="Internal")

    ones_f = nc.alloc_sbuf_tensor("ones_f", [128, 128], F32)
    ident = nc.alloc_sbuf_tensor("ident", [128, 128], F32)
    mask8 = nc.alloc_sbuf_tensor("mask8s", [128, 512], I8)
    cmb = nc.alloc_sbuf_tensor("cmb", [128, 8], F32)
    cm1b = nc.alloc_sbuf_tensor("cm1b", [128, 8], F32)
    bqt = nc.alloc_sbuf_tensor("bqt_s", [128, 16], F32)
    bqs = nc.alloc_sbuf_tensor("bqs_s", [128, 16], F32)
    bkt = nc.alloc_sbuf_tensor("bkt_s", [128, 4], F32)
    nbkt = nc.alloc_sbuf_tensor("nbkt_s", [128, 4], F32)
    bvt = nc.alloc_sbuf_tensor("bvt_s", [128, 4], F32)
    gst = nc.alloc_sbuf_tensor("gst_s", [128, 44], F32)
    ust = nc.alloc_sbuf_tensor("ust_s", [128, 44], F32)
    dst = nc.alloc_sbuf_tensor("dst_s", [128, 16], F32)
    stat1 = nc.alloc_sbuf_tensor("stat1", [128, 16], F32)
    stat2 = nc.alloc_sbuf_tensor("stat2", [128, 16], F32)
    meanv = nc.alloc_sbuf_tensor("meanv", [128, 16], F32)
    em2 = nc.alloc_sbuf_tensor("em2", [128, 16], F32)
    varb = nc.alloc_sbuf_tensor("varb", [128, 16], F32)
    lnv = nc.alloc_sbuf_tensor("lnv", [128, 16], F32)
    rstd = nc.alloc_sbuf_tensor("rstd", [128, 16], F32)
    nmr = nc.alloc_sbuf_tensor("nmr", [128, 16], F32)
    acs4 = nc.alloc_sbuf_tensor("acs4", [128, 4], F32)
    bmacs = nc.alloc_sbuf_tensor("bmacs", [128, 64], F32)
    dtot4 = nc.alloc_sbuf_tensor("dtot4", [128, 4], F32)
    eab = nc.alloc_sbuf_tensor("eab", [128, 64], F32)
    s_state = nc.alloc_sbuf_tensor("s_state", [128, 4 * 128], F32)
    spack = nc.alloc_sbuf_tensor("spack", [128, 4 * 129], F32)
    rfold = nc.alloc_sbuf_tensor("rfold", [128, 4 * 129], F32)
    dpcol4 = nc.alloc_sbuf_tensor("dpcol4", [128, 4], F32)
    epsb = nc.alloc_sbuf_tensor("epsb", [128, 1], F32)
    ebring = nc.alloc_sbuf_tensor("ebring", [128, NRING * 2], F32)
    ering = nc.alloc_sbuf_tensor("ering", [128, NRING * 512], BF16)
    kring = nc.alloc_sbuf_tensor("kring", [128, NRING * 128], BF16)
    khring = nc.alloc_sbuf_tensor("khring", [128, NRING * 128], F32)
    gbuf = nc.alloc_sbuf_tensor("gbuf", [128, 512], F32)
    bbuf3 = nc.alloc_sbuf_tensor("bbuf3", [128, 3 * 512], F32)
    nb4r = nc.alloc_sbuf_tensor("nb4r", [128, NRING * 4], F32)
    kexpb = nc.alloc_sbuf_tensor("kexpb", [128, 4 * 128], F32)
    kh2eb = nc.alloc_sbuf_tensor("kh2eb", [128, 4 * 128], F32)
    ekb = nc.alloc_sbuf_tensor("ekb", [128, 2 * 512], F32)
    kpost = nc.alloc_sbuf_tensor("kpost", [128, 512], F32)
    kpost2 = nc.alloc_sbuf_tensor("kpost2", [128, 512], F32)
    rtmp4 = nc.alloc_sbuf_tensor("rtmp4", [128, 4 * 129], F32)
    qside = nc.alloc_sbuf_tensor("qside", [128, 4 * 512], BF16)
    stil = nc.alloc_sbuf_tensor("stil", [128, 2 * 512], BF16)
    khtb = nc.alloc_sbuf_tensor("khtb", [128, 2 * 128], BF16)
    sscal = nc.alloc_sbuf_tensor("sscal", [128, 2 * 128], BF16)
    scorb = nc.alloc_sbuf_tensor("scorb", [128, 4 * 128], BF16)
    corrstg = nc.alloc_sbuf_tensor("corrstg", [128, 512], F32)
    ostg = nc.alloc_sbuf_tensor("ostg", [128, 512], F32)
    sgb = nc.alloc_sbuf_tensor("sgb", [128, 2 * 512], BF16)
    dnstg = nc.alloc_sbuf_tensor("dnstg", [128, 2 * 512], F32)
    dnpld = nc.alloc_sbuf_tensor("dnpld", [128, 2 * 512], F32)
    trstg = nc.alloc_sbuf_tensor("trstg", [128, 512], F32)
    vstg = nc.alloc_sbuf_tensor("vstg", [128, 2 * 512], F32)
    wdbuf = nc.alloc_sbuf_tensor("wdbuf", [128, 2 * 1408], BF16)
    wun = nc.alloc_sbuf_tensor("wun", [128, 8192], BF16)

    AR = (nc.sbuf_base + 63) & ~63
    assert AR + 112 * 1024 <= nc.sbuf_top, (AR, nc.sbuf_top)
    xnT = nc.alloc_sbuf_tensor_at("xnT", [128, 16 * 1024], BF16, offset=AR + 0)
    qs = nc.alloc_sbuf_tensor_at("qs", [128, 16 * 1024], BF16, offset=AR + 32 * 1024)
    kf = nc.alloc_sbuf_tensor_at("kf", [128, 4 * 1024], F32, offset=AR + 64 * 1024)
    vtok = nc.alloc_sbuf_tensor_at("vtok", [128, 8 * 512], BF16, offset=AR + 80 * 1024)
    qhld = nc.alloc_sbuf_tensor_at("qhld", [128, 4 * 512], BF16, offset=AR + 84 * 1024)
    hbuf = nc.alloc_sbuf_tensor_at("hbuf", [128, NICP * 1024], BF16, offset=AR + 88 * 1024)
    xn_tok = nc.alloc_sbuf_tensor_at("xn_tok", [128, 2048], F32, offset=AR + 88 * 1024)
    xbuf = nc.alloc_sbuf_tensor_at("xbuf", [128, 2 * 2048], F32, offset=AR + 96 * 1024)
    fsum = nc.alloc_sbuf_tensor_at("fsum", [128, 4 * 2048], F32, offset=AR + 32 * 1024)
    sqscr = nc.alloc_sbuf_tensor_at("sqscr", [128, 2048], F32, offset=AR + 32 * 1024)
    cstg = nc.alloc_sbuf_tensor_at("cstg", [128, 16 * 512], F32, offset=AR + 0)
    ccst = nc.alloc_sbuf_tensor_at("ccst", [128, 32 * 129], F32, offset=AR + 64 * 1024)

    ps_mm = [nc.alloc_psum_tensor(f"ps_mm{i}", [128, 512], F32) for i in range(4)]
    ps_sc = nc.alloc_psum_tensor("ps_sc", [128, 512], F32)
    ps_o = [nc.alloc_psum_tensor(f"ps_o{i}", [128, 512], F32) for i in range(2)]
    ps_tr = nc.alloc_psum_tensor("ps_tr", [128, 512], F32)
    trb = [ps_tr, ps_o[0], ps_o[1]]

    S = {}
    for name in [
        "ini", "prep", "init0", "lna", "var", "rstd", "nmr", "xn",
        "trp", "trev", "mmp",
        "ek", "kfs", "vfms", "qss", "agn", "aarg", "aexp", "aprep",
        "qsd", "scj", "stl", "trk", "khe", "umm", "ssc", "omm", "oev",
        "aring", "mmgu", "silu", "hcmb",
        "mmdn", "dnev", "trq", "trqe",
        "pack", "dtot", "ccw", "cc", "ccl", "fold", "ea", "scorr",
        "fmm", "fsum", "cev",
        "xld0", "xld1", "wldp0", "wldp1", "wldgu0", "wldgu1",
        "wldd0", "wldd1", "dnpld0", "dnpld1", "dnpsp0", "dnpsp1",
        "osp0", "osp1", "qhsp0", "qhsp1", "dnsp0", "dnsp1",
        "fqh0", "fqh1", "fqh2", "fqh3", "cacc0", "cacc1", "fx0", "fx1", "facc0", "facc1",
        "facc2", "facc3",
        "outs0", "outs1", "outs2", "outs3", "vv", "av",
    ]:
        S[name] = nc.alloc_semaphore("sem_" + name)

    def P(fam, idx):
        return S[f"{fam}{idx % 2}"]

    PROJ = ([("k", oc, j) for oc in range(4) for j in range(2)]
            + [("v", oc, j) for oc in range(4) for j in range(2)]
            + [("q", oc, j) for oc in range(16) for j in range(2)])

    ORDER = []
    for tau in range(NT):
        for pg in range(48):
            ORDER.append(("proj", tau, pg))
        for it in PLANS[tau]:
            if it[0] in ("gu", "dn", "tr"):
                ORDER.append((tau,) + it)
    for _uq in range(64):
        ORDER.append((1, "corr", _uq))
    G2 = {k: i for i, k in enumerate(ORDER)}

    def gid(key):
        return G2[key]

    evrec = {}
    for key in ORDER:
        if key[0] == "proj":
            _, tau, pg = key
            fam, oc, j = PROJ[pg]
            if fam == "k":
                evrec[G2[key]] = ("ek", tau * 8 + oc * 2 + j + 1)
            elif fam == "v":
                evrec[G2[key]] = ("vfms", tau * 8 + oc * 2 + j + 1)
            else:
                evrec[G2[key]] = ("qss", tau * 32 + (pg - 16) + 1)
        else:
            tau = key[0]
            kind = key[1]
            if kind == "gu":
                _, _, part, icH, sub = key
                hc = tau * 88 + part * 22 + icH * 2 + (sub // 2)
                evrec[G2[key]] = ("silu" if sub % 2 == 0 else "hcmb", hc + 1)
            elif kind == "dn":
                _, _, part, ib = key
                evrec[G2[key]] = ("dnev", tau * 128 + part * 32 + ib + 1)
            elif kind == "tr":
                _, _, i = key
                evrec[G2[key]] = ("trqe", tau * 32 + i + 1)
            else:
                _, _, uq = key
                evrec[G2[key]] = ("cev", uq + 1)

    def bank_wait(eng, gidx):
        if gidx >= 4:
            semn, val = evrec[gidx - 4]
            eng.wait_ge(S[semn], val)

    def pslab_of(fam, oc):
        return {"k": 0, "v": 4, "q": 8}[fam] + oc

    def dnstg_free_wait(eng, tau, evg):
        pe = tau * 128 + evg - 2
        if pe < 0:
            return
        ptau, pevg = pe // 128, pe % 128
        pp, pib = pevg // 32, pevg % 32
        if pp < 3:
            spidx = ptau * 96 + pp * 32 + pib
            eng.wait_ge(P("dnpsp", spidx), 16 * (spidx // 2 + 1))
        else:
            eng.wait_ge(S["trq"], ptau * 32 + pib + 1)

    VV = [0]
    AV = [0]

    bg = nc.Block()
    blk = bg.__enter__()

    @blk.gpsimd
    def _(g):
        FSTEPS = []
        for fci in range(8):
            FSTEPS += [("x", fci), ("o", fci), ("dn", fci), ("st", fci)]
        for fci in range(8, 12):
            FSTEPS += [("x", fci), ("o", fci)]
        NPRE = len(FSTEPS)
        FSTEPS += [("acc", fci) for fci in range(4)]
        FSTEPS += [("dn", fci) for fci in range(8, 12)]
        FSTEPS += [("st", fci) for fci in range(8, 12)]
        FSTEPS += [("acc", fci) for fci in range(4, 8)]
        FSTEPS += [("x", fci) for fci in range(12, 16)]
        FSTEPS += [("o", fci) for fci in range(12, 16)]
        FSTEPS += [("dn", fci) for fci in range(12, 16)]
        FSTEPS += [("st", fci) for fci in range(12, 16)]
        FSTEPS += [("acc", fci) for fci in range(8, 16)]
        FPOS = [0]

        def emit_fstep(limit=None):
            if FPOS[0] >= (len(FSTEPS) if limit is None else limit):
                return
            knd, ci = FSTEPS[FPOS[0]]
            FPOS[0] += 1
            tau, c = ci // 8, ci % 8
            base = 64 * (ci // 4)
            fam = S[f"facc{ci % 4}"]
            slot = fsum[:, (ci % 4) * 2048:(ci % 4) * 2048 + 2048]
            rows = out_h[ci * 128:(ci + 1) * 128, :]
            if knd == "x":
                if ci == 0:
                    g.wait_ge(S["qsd"], 256)
                if ci >= 4:
                    g.wait_ge(fam, base)
                g.dma_start(out=slot, in_=x_h[ci * 128:(ci + 1) * 128, :]
                            ).then_inc(fam, 16)
            elif knd == "o":
                ug = tau * NU + c * 4 + 3
                g.wait_ge(P("osp", ug), 16 * (ug // 2 + 1))
                g.wait_ge(P("osp", ug - 1), 16 * ((ug - 1) // 2 + 1))
                g.wait_ge(fam, base + 16)
                g.dma_start(out=slot, in_=o_scr[ci, :, :],
                            accum_op=ALU.add).then_inc(fam, 16)
            elif knd == "dn":
                tq = tau * 32 + 30 + (c // 4)
                g.wait_ge(P("dnsp", tq), 16 * (tq // 2 + 1))
                g.wait_ge(fam, base + 32)
                g.dma_start(out=slot, in_=dn_scr[ci, :, :],
                            accum_op=ALU.add).then_inc(fam, 16)
            elif knd == "st":
                g.wait_ge(fam, base + 48)
                g.dma_start(out=rows, in_=slot).then_inc(fam, 16)
            else:
                g.wait_ge(S["cev"], 4 * (ci + 1))
                g.wait_ge(fam, base + 64)
                g.dma_start(out=rows,
                            in_=cstg[:, (ci % 4) * 2048:(ci % 4) * 2048 + 2048],
                            accum_op=ALU.add).then_inc(S[f"outs{ci % 4}"], 16)

        g.memset(ones_f[:, :], 1.0)
        g.memset(s_state[:, :], 0.0)
        g.memset(acs4[:, :], 0.0)
        g.memset(rfold[:, :], 0.0)
        g.memset(stil[:, :], 0)
        g.memset(epsb[:, :], 1e-5).then_inc(S["init0"], 1)

        for tau in range(NT):
            for ws in range(24):
                src, row0 = ((wk_h, ws) if ws < 4 else
                             (wv_h, ws - 4) if ws < 8 else (wq_h, ws - 8))
                wsg = tau * 24 + ws
                if wsg >= 2:
                    g.wait_ge(S["mmp"], (wsg - 2) * 2 + 2)
                if tau == 1 and ws < 2:
                    g.wait_ge(S["mmgu"], (42 + ws) * 4 + 4)
                for hh in range(2):
                    g.dma_start(
                        out=wun[:, (ws % 2) * 2048 + hh * 1024:(ws % 2) * 2048 + hh * 1024 + 1024],
                        in_=src[row0, :, hh * 1024:hh * 1024 + 1024],
                    ).then_inc(P("wldp", wsg), 16)
            for part in range(NPART):
                for icH in range(NICP):
                    gl = part * NICP + icH
                    icg = tau * 44 + gl
                    if gl < 2:
                        if tau == 0 and gl == 0:
                            g.wait_ge(S["mmp"], 48)
                        if tau == 1:
                            g.wait_ge(S["mmp"], 96)
                            g.wait_ge(S["mmgu"], (42 + gl) * 4 + 4)
                    else:
                        g.wait_ge(S["mmgu"], tau * 176 + (gl - 2) * 4 + 4)
                    g.dma_start(
                        out=wun[:, (gl % 2) * 4096:(gl % 2) * 4096 + 4096],
                        in_=wgu_h[part * NICP + icH, :, :],
                    ).then_inc(P("wldgu", icg), 16)
                    if tau == 1 and part >= 2:
                        emit_fstep(NPRE)
                for dmc in range(16):
                    dsl = tau * 64 + part * 16 + dmc
                    if dsl >= 2:
                        pd = dsl - 2
                        ptau, prest = pd // 64, pd % 64
                        ppart, pdmc = prest // 16, prest % 16
                        g.wait_ge(S["mmdn"],
                                  ptau * 128 + ppart * 32 + pdmc * 2 + 2)
                    g.dma_start(
                        out=wdbuf[:, (dmc % 2) * 1408:(dmc % 2) * 1408 + 1408],
                        in_=bass.AP(wd_h, part * NICP * 128 * 2048 + dmc * 128,
                                    [[2048, 128], [128 * 2048, NICP], [1, 128]]),
                    ).then_inc(P("wldd", dsl), 16)
                    if tau == 1 and part >= 2:
                        emit_fstep(NPRE)
                if tau == 1 and part == 1:
                    g.wait_ge(S["ccw"], 64)
                    g.collective_compute(
                        "AllGather", ALU.bypass,
                        replica_groups=[list(range(NCORE))],
                        ins=[cc_in.ap().opt()],
                        outs=[cc_out.ap().opt()],
                    ).then_inc(S["cc"], 1)

        while FPOS[0] < len(FSTEPS):
            emit_fstep()

    @blk.sync
    def _(s):
        for src, dstb in [(bqt_h, bqt), (bkt_h, bkt), (bvt_h, bvt),
                          (gst_h, gst), (ust_h, ust), (dst_h, dst),
                          (idn_h, ident), (msk_h, mask8), (cm_h, cmb),
                          (cm1_h, cm1b)]:
            s.dma_start(out=dstb[:, :], in_=src[:, :]).then_inc(S["ini"], 16)

        for tau in range(NT):
            for sub in range(NS):
                ts = tau * NS + sub
                if ts >= 2:
                    s.wait_ge(S["xn"], ts - 1)
                    s.wait_ge(S["lna"], ts - 1)
                if tau >= 1 and sub == 0:
                    s.wait_ge(S["mmdn"], 128 * tau)
                for hh in range(2):
                    s.dma_start(out=xbuf[:, (ts % 2) * 2048 + hh * 1024:(ts % 2) * 2048 + hh * 1024 + 1024],
                                in_=x_h[ts * 128:(ts + 1) * 128, hh * 1024:hh * 1024 + 1024]
                                ).then_inc(P("xld", ts), 16)

            for it in PLANS[tau]:
                kind = it[0]
                if kind == "attn":
                    u = it[1]
                    ug = tau * NU + u
                    ci = tau * NCH + u // 4
                    gi = u % 4
                    s.wait_ge(S["qsd"], ug * 4 + 1)
                    s.dma_start(out=qh_scr[ci, gi, :, :],
                                in_=qside[:, 0:512]).then_inc(P("qhsp", ug), 16)
                    s.wait_ge(S["oev"], ug + 1)
                    s.dma_start(out=bass.AP(o_scr, ci * 128 * 2048 + gi * 512,
                                            [[2048, 128], [1, 512]]),
                                in_=ostg[:, 0:512]).then_inc(P("osp", ug), 16)
                elif kind == "dn":
                    part, ib = it[1], it[2]
                    dmc, j = ib // 2, ib % 2
                    dnn = part * 32 + ib
                    if tau == 1 and part == 2 and ib < 16:
                        if ib == 0:
                            s.wait_ge(S["cc"], 1)
                            s.wait_ge(S["kfs"], 16)
                            s.wait_ge(S["aprep"], 64)
                            s.wait_ge(S["aring"], 64)
                        for r in range(ib * 2, ib * 2 + 2):
                            s.dma_start(out=ccst[:, r * 129:(r + 1) * 129],
                                        in_=cc_out[r, :, :]).then_inc(S["ccl"], 16)
                    ev_i = tau * 128 + dnn
                    if part < 3:
                        spidx = tau * 96 + dnn
                        s.wait_ge(S["dnev"], ev_i + 1)
                        s.dma_start(
                            out=bass.AP(dnp_scr, dmc * 128 * 2048 + tau * 1024 + j * 512,
                                        [[2048, 128], [1, 512]]),
                            in_=dnstg[:, (ev_i % 2) * 512:(ev_i % 2) * 512 + 512],
                        ).then_inc(P("dnpsp", spidx), 16)
                    rid_l = dnn - 30
                    if 0 <= rid_l < 96:
                        rid = tau * 96 + rid_l
                        rdmc, rj = (rid_l % 32) // 2, rid_l % 2
                        if rid_l >= 2:
                            s.wait_ge(S["dnev"], tau * 128 + rid_l + 31)
                        s.wait_ge(P("dnpsp", rid), 16 * (rid // 2 + 1))
                        s.dma_start(
                            out=dnpld[:, (rid % 2) * 512:(rid % 2) * 512 + 512],
                            in_=bass.AP(dnp_scr, rdmc * 128 * 2048 + tau * 1024 + rj * 512,
                                        [[2048, 128], [1, 512]]),
                        ).then_inc(P("dnpld", rid), 16)
                elif kind == "tr":
                    i = it[1]
                    tq = tau * 32 + i
                    dmc, j = i // 2, i % 2
                    s.wait_ge(S["trqe"], tq + 1)
                    s.dma_start(
                        out=bass.AP(dn_scr,
                                    (tau * 8 + j * 4) * 128 * 2048 + dmc * 128,
                                    [[2048, 128], [128 * 2048, 4], [1, 128]]),
                        in_=trstg[:, 0:512],
                    ).then_inc(P("dnsp", tq), 16)
                elif kind == "cc":
                    s.wait_ge(S["pack"], 1)
                    for gi in range(4):
                        s.dma_start(out=cc_in[gi, :, :],
                                    in_=spack[:, gi * 129:(gi + 1) * 129]
                                    ).then_inc(S["ccw"], 16)

        for pp in range(2):
            s.wait_ge(S[f"qhsp{pp}"], 16 * 32)
        for uq in range(64):
            if uq >= 4:
                s.wait_ge(S["fmm"], uq - 3)
            s.dma_start(out=qhld[:, (uq % 4) * 512:(uq % 4) * 512 + 512],
                        in_=qh_scr[uq // 4, uq % 4, :, :]
                        ).then_inc(S[f"fqh{uq % 4}"], 16)
        for ff in range(4):
            s.wait_ge(S[f"outs{ff}"], 16 * 4)

    @blk.tensor
    def _(t):
        t.wait_ge(S["ini"], 160)

        def emit_attn(u, tau):
            ug = tau * NU + u
            ci = u // 4
            gi = u % 4
            slot = ug % NRING
            t.wait_ge(S["aring"], ug)
            t.wait_ge(S["aprep"], ug + 1)
            t.wait_ge(S["trev"], (tau + 1) * 160)
            for j in range(4):
                t.wait_ge(S["qsd"], ug * 4 + j + 1)
                t.matmul(ps_sc[32 * j:32 * (j + 1), :],
                         kring[:, slot * 128 + 32 * j:slot * 128 + 32 * j + 32],
                         qside[:, j * 512:(j + 1) * 512],
                         start=True, stop=True,
                         tile_position=(0, 32 * j)).then_inc(S["scj"], 1)
            t.wait_ge(S["stl"], ug + 1)
            t.transpose(ps_sc[:, 0:128], khring[:, slot * 128:slot * 128 + 128],
                        ident[:, :]).then_inc(S["trk"], 1)
            t.wait_ge(S["khe"], ug + 1)
            t.matmul(ps_sc[:, 0:128],
                     khtb[:, (ug % 2) * 128:(ug % 2) * 128 + 128],
                     vtok[:, ci * 512 + gi * 128:ci * 512 + gi * 128 + 128],
                     start=True, stop=True).then_inc(S["umm"], 1)
            t.wait_ge(S["ssc"], ug + 1)
            if ug >= 2:
                t.wait_ge(S["oev"], ug - 1)
            last = None
            for hh in range(4):
                t.matmul(ps_o[ug % 2][:, hh * 128:(hh + 1) * 128],
                         stil[:, (ug % 2) * 512 + hh * 128:(ug % 2) * 512 + hh * 128 + 128],
                         vtok[:, ci * 512 + gi * 128:ci * 512 + gi * 128 + 128],
                         start=True, stop=False)
                last = t.matmul(ps_o[ug % 2][:, hh * 128:(hh + 1) * 128],
                                qside[:, hh * 128:hh * 128 + 128],
                                sscal[:, (ug % 2) * 128:(ug % 2) * 128 + 128],
                                start=False, stop=True)
            last.then_inc(S["omm"], 1)

        for tau in range(NT):
            for sub in range(NS):
                ts = tau * NS + sub
                t.wait_ge(S["xn"], ts + 1)
                for k in range(16):
                    trn = tau * 160 + sub * 16 + k
                    if trn >= 3:
                        t.wait_ge(S["trev"], trn - 2)
                    if tau >= 1 and sub == 0 and k == 0:
                        t.wait_ge(S["oev"], tau * NU)
                        t.wait_ge(S["aring"], tau * NU)
                    t.transpose(trb[trn % 3][:, 0:128],
                                xn_tok[:, k * 128:(k + 1) * 128],
                                ident[:, :]).then_inc(S["trp"], 1)
            for pg, (fam, oc, j) in enumerate(PROJ):
                gidx = gid(("proj", tau, pg))
                ws = pslab_of(fam, oc)
                wsg = tau * 24 + ws
                t.wait_ge(P("wldp", wsg), 32 * (wsg // 2 + 1))
                t.wait_ge(S["trev"], tau * 160 + 64 * (j + 1))
                bank_wait(t, gidx)
                bank = ps_mm[gidx % 4]
                last = None
                for kk in range(16):
                    last = t.matmul(
                        bank[:, :],
                        wun[:, (ws % 2) * 2048 + kk * 128:(ws % 2) * 2048 + kk * 128 + 128],
                        xnT[:, kk * 1024 + j * 512:kk * 1024 + j * 512 + 512],
                        start=(kk == 0), stop=(kk == 15))
                last.then_inc(S["mmp"], 1)
                if fam == "v":
                    vg = oc * 2 + j
                    for b in range(4):
                        trn = tau * 160 + 128 + vg * 4 + b
                        t.wait_ge(S["vfms"], tau * 8 + vg + 1)
                        if trn >= 3:
                            t.wait_ge(S["trev"], trn - 2)
                        t.transpose(trb[trn % 3][:, 0:128],
                                    vstg[:, (vg % 2) * 512 + b * 128:(vg % 2) * 512 + b * 128 + 128],
                                    ident[:, :]).then_inc(S["trp"], 1)
            for it in PLANS[tau]:
                kind = it[0]
                if kind == "attn":
                    emit_attn(it[1], tau)
                elif kind == "gu":
                    part, icH, sub = it[1], it[2], it[3]
                    gl = part * NICP + icH
                    icg = tau * 44 + gl
                    gu = sub % 2
                    j = sub // 2
                    if sub == 0:
                        t.wait_ge(P("wldgu", icg), 16 * (icg // 2 + 1))
                    gidx = gid((tau,) + it)
                    bank_wait(t, gidx)
                    bank = ps_mm[gidx % 4]
                    last = None
                    for kk in range(16):
                        last = t.matmul(
                            bank[:, :],
                            wun[:, (gl % 2) * 4096 + gu * 2048 + kk * 128:
                                 (gl % 2) * 4096 + gu * 2048 + kk * 128 + 128],
                            xnT[:, kk * 1024 + j * 512:kk * 1024 + j * 512 + 512],
                            start=(kk == 0), stop=(kk == 15))
                    last.then_inc(S["mmgu"], 1)
                elif kind == "dn":
                    part, ib = it[1], it[2]
                    dmc, j = ib // 2, ib % 2
                    dsl = tau * 64 + part * 16 + dmc
                    t.wait_ge(P("wldd", dsl), 16 * (dsl // 2 + 1))
                    t.wait_ge(S["hcmb"], tau * 88 + part * 22 + 21 + j)
                    gidx = gid((tau,) + it)
                    bank_wait(t, gidx)
                    bank = ps_mm[gidx % 4]
                    last = None
                    for icH in range(NICP):
                        last = t.matmul(
                            bank[:, :],
                            wdbuf[:, (dmc % 2) * 1408 + icH * 128:(dmc % 2) * 1408 + icH * 128 + 128],
                            hbuf[:, icH * 1024 + j * 512:icH * 1024 + j * 512 + 512],
                            start=(icH == 0), stop=(icH == NICP - 1))
                    last.then_inc(S["mmdn"], 1)
                elif kind == "tr":
                    i = it[1]
                    ev_i = tau * 128 + 96 + i
                    t.wait_ge(S["dnev"], ev_i + 1)
                    gidx = gid((tau,) + it)
                    bank_wait(t, gidx)
                    bank = ps_mm[gidx % 4]
                    last = None
                    for b in range(4):
                        last = t.transpose(
                            bank[:, b * 128:(b + 1) * 128],
                            dnstg[:, (ev_i % 2) * 512 + b * 128:(ev_i % 2) * 512 + b * 128 + 128],
                            ident[:, :])
                    last.then_inc(S["trq"], 1)

        for uq in range(64):
            t.wait_ge(S[f"fqh{uq % 4}"], 16 * (uq // 4 + 1))
            t.wait_ge(S["scorr"], uq + 1)
            gidx = gid((1, "corr", uq))
            bank_wait(t, gidx)
            bank = ps_mm[gidx % 4]
            last = None
            for hh in range(4):
                last = t.matmul(bank[:, hh * 128:(hh + 1) * 128],
                                qhld[:, (uq % 4) * 512 + hh * 128:(uq % 4) * 512 + hh * 128 + 128],
                                scorb[:, (uq % 4) * 128:(uq % 4) * 128 + 128],
                                start=True, stop=True)
            last.then_inc(S["fmm"], 1)

    @blk.scalar
    def _(a):
        a.wait_ge(S["ini"], 160)
        a.wait_ge(S["prep"], 1)

        def emit_burst(tau, c):
            if c + 1 < NCH:
                c2 = c + 1
                a.wait_ge(S["kfs"], tau * 8 + 7 + c2 // 4)
                a.wait_ge(S["aarg"], tau * NU + c2 * 4)
                a.activation(out=gbuf[:, :],
                             in_=bass.AP(kf, c2 * 128, [[4096, 128], [1024, 4], [1, 128]]),
                             func=AF.Ln, scale=-1.0,
                             bias=ones_f[:, 0:1]).then_inc(S["agn"], 1)
            C = tau * NCH + c
            bb = (C % 3) * 512
            for gg in range(4):
                ug = tau * NU + c * 4 + gg
                a.wait_ge(S["aarg"], ug + 1)
                if ug >= 4:
                    a.wait_ge(S["aprep"], ug - 3)
                if ug >= NRING:
                    a.wait_ge(S["aring"], ug - NRING + 1)
                slot = ug % NRING
                ks4 = (ug % 4) * 128
                bsl = bbuf3[:, bb + gg * 128:bb + gg * 128 + 128]
                for j in range(4):
                    a.activation(out=ering[:, slot * 512 + j * 128:slot * 512 + j * 128 + 128],
                                 in_=bsl, func=AF.Exp,
                                 bias=nb4r[:, slot * 4 + j:slot * 4 + j + 1])
                for j in range(4):
                    a.activation(out=kexpb[:, ks4 + 32 * j:ks4 + 32 * j + 32],
                                 in_=bbuf3[:, bb + gg * 128 + 32 * j:bb + gg * 128 + 32 * j + 32],
                                 func=AF.Exp, scale=-1.0,
                                 bias=bbuf3[:, bb + gg * 128 + 32 * j + 15:bb + gg * 128 + 32 * j + 16])
                a.activation(out=kh2eb[:, ks4:ks4 + 128], in_=bsl, func=AF.Exp,
                             scale=-1.0,
                             bias=bbuf3[:, bb + gg * 128 + 127:bb + gg * 128 + 128])
                a.activation(out=ebring[:, slot * 2:slot * 2 + 2],
                             in_=bass.AP(bbuf3, bb + gg * 128 + 15, [[3 * 512, 128], [112, 2]]),
                             func=AF.Exp).then_inc(S["aexp"], 1)

        for tau in range(NT):
            for sub in range(NS + 1):
                if sub >= 1:
                    psub = sub - 1
                    for k in range(16):
                        trn = tau * 160 + psub * 16 + k
                        a.wait_ge(S["trp"], trn + 1)
                        if tau >= 1 and psub == 0 and k == 0:
                            a.wait_ge(S["mmgu"], tau * 176)
                        a.activation(
                            out=xnT[:, k * 1024 + psub * 128:k * 1024 + psub * 128 + 128],
                            in_=trb[trn % 3][:, 0:128],
                            func=AF.Copy).then_inc(S["trev"], 1)
                if sub >= NS:
                    continue
                ts = tau * NS + sub
                a.wait_ge(S["var"], ts + 1)
                AV[0] += 1
                a.activation(out=lnv[:, ts:ts + 1], in_=varb[:, ts:ts + 1],
                             func=AF.Ln, bias=epsb[:, 0:1]).then_inc(S["av"], 1)
                a.wait_ge(S["av"], AV[0])
                a.activation(out=rstd[:, ts:ts + 1], in_=lnv[:, ts:ts + 1],
                             func=AF.Exp, scale=-0.5).then_inc(S["rstd"], 1)
                a.wait_ge(S["nmr"], ts + 1)
                a.wait_ge(S["rstd"], ts + 1)
                if ts >= 1:
                    a.wait_ge(S["trp"], tau * 160 + sub * 16)
                a.activation(out=xn_tok[:, :],
                             in_=xbuf[:, (ts % 2) * 2048:(ts % 2) * 2048 + 2048],
                             func=AF.Identity, scale=rstd[:, ts:ts + 1],
                             bias=nmr[:, ts:ts + 1]).then_inc(S["xn"], 1)
            for pg, (fam, oc, j) in enumerate(PROJ):
                gidx = gid(("proj", tau, pg))
                bank = ps_mm[gidx % 4]
                a.wait_ge(S["mmp"], tau * 48 + pg + 1)
                if fam == "k":
                    kp = tau * 8 + oc * 2 + j
                    if kp >= 2:
                        a.wait_ge(S["kfs"], kp - 1)
                    a.activation(out=ekb[:, (kp % 2) * 512:(kp % 2) * 512 + 512],
                                 in_=bank[:, :], func=AF.Exp,
                                 scale=-1.0, bias=nbkt[:, oc:oc + 1]
                                 ).then_inc(S["ek"], 1)
                elif fam == "v":
                    vg = oc * 2 + j
                    if vg >= 2:
                        a.wait_ge(S["trp"], tau * 160 + 128 + 4 * vg - 4)
                    elif tau >= 1:
                        a.wait_ge(S["trp"], (tau - 1) * 160 + 128 + 4 * (vg + 6) + 4)
                    a.activation(out=vstg[:, (vg % 2) * 512:(vg % 2) * 512 + 512],
                                 in_=bank[:, :], func=AF.Identity,
                                 bias=bvt[:, oc:oc + 1]).then_inc(S["vfms"], 1)
                else:
                    if tau >= 1 and oc == 0 and j == 0:
                        a.wait_ge(S["qsd"], tau * 128)
                    a.activation(out=qs[:, oc * 1024 + j * 512:oc * 1024 + j * 512 + 512],
                                 in_=bank[:, :], func=AF.Relu, scale=SCALE,
                                 bias=bqs[:, oc:oc + 1]).then_inc(S["qss"], 1)
            a.wait_ge(S["kfs"], tau * 8 + 7)
            if tau >= 1:
                a.wait_ge(S["aarg"], tau * NU)
            a.activation(out=gbuf[:, :],
                         in_=bass.AP(kf, 0, [[4096, 128], [1024, 4], [1, 128]]),
                         func=AF.Ln, scale=-1.0,
                         bias=ones_f[:, 0:1]).then_inc(S["agn"], 1)
            emit_burst(tau, 0)
            for it in PLANS[tau]:
                kind = it[0]
                if kind == "attn" and it[1] % 4 == 0 and it[1] >= 4:
                    emit_burst(tau, it[1] // 4)
                elif kind == "cc":
                    a.wait_ge(S["aarg"], 2 * NU)
                    a.activation(out=dtot4[:, :], in_=acs4[:, :],
                                 func=AF.Exp).then_inc(S["dtot"], 1)
                    a.activation(out=eab[:, :], in_=bmacs[:, :],
                                 func=AF.Exp).then_inc(S["ea"], 1)
                elif kind == "gu" and it[3] % 2 == 0:
                    part, icH, sub = it[1], it[2], it[3]
                    j = sub // 2
                    ic = part * NICP + icH
                    hc = tau * 88 + part * 22 + icH * 2 + j
                    a.wait_ge(S["mmgu"], tau * 176 + ic * 4 + sub + 1)
                    if hc >= 2:
                        a.wait_ge(S["hcmb"], hc - 1)
                    a.activation(out=sgb[:, (j % 2) * 512:(j % 2) * 512 + 512],
                                 in_=ps_mm[gid((tau,) + it) % 4][:, :], func=AF.Silu,
                                 scale=gst[:, ic:ic + 1]).then_inc(S["silu"], 1)

    @blk.vector
    def _(v):
        v.wait_ge(S["ini"], 160)
        v.wait_ge(S["init0"], 1)
        v.tensor_scalar_mul(bqs[:, :], bqt[:, :], SCALE)
        v.tensor_scalar_mul(nbkt[:, :], bkt[:, :], -1.0).then_inc(S["prep"], 1)

        def emit_scan(c, tau):
            C = tau * NCH + c
            bb = (C % 3) * 512
            v.wait_ge(S["agn"], C + 1)
            if C >= 3:
                v.wait_ge(S["aexp"], 4 * C - 8)
            VV[0] += 1
            for g2 in range(4):
                last_scan = v.tensor_tensor_scan(
                    out=bbuf3[:, bb + g2 * 128:bb + (g2 + 1) * 128],
                    data0=ones_f[:, :],
                    data1=gbuf[:, g2 * 128:(g2 + 1) * 128],
                    initial=0.0, op0=ALU.mult, op1=ALU.add)
            last_scan.then_inc(S["vv"], 1)
            v.wait_ge(S["vv"], VV[0])

        def emit_args(u, tau):
            ug = tau * NU + u
            c = u // 4
            gg = u % 4
            C = tau * NCH + c
            bb = (C % 3) * 512
            if ug >= NRING:
                v.wait_ge(S["aexp"], ug - NRING + 1)
            if ug >= 4:
                v.wait_ge(S["aarg"], ug - 3)
            slot = ug % NRING
            v.tensor_scalar_mul(nb4r[:, slot * 4:slot * 4 + 4],
                                bass.AP(bbuf3, bb + gg * 128 + 15, [[3 * 512, 128], [32, 4]]),
                                -1.0)
            VV[0] += 1
            v.tensor_add(bmacs[:, ug:ug + 1], acs4[:, gg:gg + 1],
                         bbuf3[:, bb + gg * 128 + 15:bb + gg * 128 + 16]).then_inc(S["vv"], 1)
            v.wait_ge(S["vv"], VV[0])
            v.tensor_add(acs4[:, gg:gg + 1], acs4[:, gg:gg + 1],
                         bbuf3[:, bb + gg * 128 + 127:bb + gg * 128 + 128]).then_inc(S["aarg"], 1)

        def emit_attn_v(u, tau):
            ug = tau * NU + u
            c = u // 4
            gg = u % 4
            slot = ug % NRING
            if ug >= 4:
                v.wait_ge(S["aring"], ug - 3)
            v.wait_ge(S["aexp"], ug + 1)
            ks4 = (ug % 4) * 128
            kfsl = kf[:, gg * 1024 + c * 128:gg * 1024 + c * 128 + 128]
            v.tensor_mul(kring[:, slot * 128:(slot + 1) * 128], kfsl,
                         kexpb[:, ks4:ks4 + 128])
            v.tensor_mul(khring[:, slot * 128:(slot + 1) * 128], kfsl,
                         kh2eb[:, ks4:ks4 + 128]).then_inc(S["aprep"], 1)
            qsrc = bass.AP(qs, gg * 4 * 1024 + c * 128, [[16384, 128], [1024, 4], [1, 128]])
            v.wait_ge(S["qss"], tau * 32 + 8 * gg + 7 + c // 4)
            for j in range(4):
                if ug >= 1:
                    v.wait_ge(S["scj"], (ug - 1) * 4 + j + 1)
                    if j == 0:
                        v.wait_ge(S["omm"], ug)
                        v.wait_ge(P("qhsp", ug - 1), 16 * ((ug - 1) // 2 + 1))
                v.tensor_mul(
                    qside[:, j * 512:(j + 1) * 512],
                    qsrc,
                    bass.AP(ering, slot * 512 + j * 128, [[NRING * 512, 128], [0, 4], [1, 128]]),
                ).then_inc(S["qsd"], 1)
            v.tensor_scalar_mul(sscal[:, (ug % 2) * 128:(ug % 2) * 128 + 128],
                                s_state[:, gg * 128:(gg + 1) * 128],
                                ebring[:, slot * 2:slot * 2 + 1]).then_inc(S["ssc"], 1)
            v.wait_ge(S["scj"], ug * 4 + 4)
            v.copy_predicated(out=stil[:, (ug % 2) * 512:(ug % 2) * 512 + 512],
                              mask=mask8[:, :],
                              data=ps_sc[:, :]).then_inc(S["stl"], 1)
            v.wait_ge(S["trk"], ug + 1)
            v.tensor_copy(khtb[:, (ug % 2) * 128:(ug % 2) * 128 + 128],
                          ps_sc[:, 0:128]).then_inc(S["khe"], 1)
            v.wait_ge(S["umm"], ug + 1)
            v.scalar_tensor_tensor(
                out=s_state[:, gg * 128:(gg + 1) * 128],
                in0=s_state[:, gg * 128:(gg + 1) * 128],
                scalar=ebring[:, slot * 2 + 1:slot * 2 + 2],
                in1=ps_sc[:, 0:128],
                op0=ALU.mult, op1=ALU.add).then_inc(S["aring"], 1)
            v.wait_ge(S["omm"], ug + 1)
            if ug >= 1:
                v.wait_ge(P("osp", ug - 1), 16 * ((ug - 1) // 2 + 1))
            v.tensor_copy(ostg[:, 0:512],
                          ps_o[ug % 2][:, :]).then_inc(S["oev"], 1)
            w = u + 5
            if w < NU:
                if w % 4 == 0:
                    emit_scan(w // 4, tau)
                emit_args(w, tau)

        def emit_fold(jj):
            if jj == 0:
                v.wait_ge(S["ccl"], 16 * 32)
            for jj in (jj,):
                    ii = None
                    for gg in range(4):
                        r = jj * 4 + gg
                        ii = v.tensor_scalar(dpcol4[:, gg:gg + 1],
                                             ccst[:, r * 129:r * 129 + 1],
                                             cmb[:, jj:jj + 1],
                                             cm1b[:, jj:jj + 1], ALU.mult, ALU.add)
                    VV[0] += 1
                    ii.then_inc(S["vv"], 1)
                    v.wait_ge(S["vv"], VV[0])
                    for gg in range(4):
                        ii = v.tensor_scalar(rtmp4[:, gg * 129:(gg + 1) * 129],
                                             rfold[:, gg * 129:(gg + 1) * 129],
                                             dpcol4[:, gg:gg + 1], None, ALU.mult)
                    VV[0] += 1
                    ii.then_inc(S["vv"], 1)
                    v.wait_ge(S["vv"], VV[0])
                    for gg in range(4):
                        r = jj * 4 + gg
                        ii = v.scalar_tensor_tensor(
                            out=rfold[:, gg * 129:(gg + 1) * 129],
                            in0=ccst[:, r * 129:(r + 1) * 129],
                            scalar=cmb[:, jj:jj + 1],
                            in1=rtmp4[:, gg * 129:(gg + 1) * 129],
                            op0=ALU.mult, op1=ALU.add)
                    VV[0] += 1
                    ii.then_inc(S["vv"], 1)
                    v.wait_ge(S["vv"], VV[0])
            if jj == 7:
                v.memset(dpcol4[:, 0:1], 0.0).then_inc(S["fold"], 1)

        for tau in range(NT):
            for sub in range(NS):
                if True:
                    ts = tau * NS + sub
                    v.wait_ge(P("xld", ts), 32 * (ts // 2 + 1))
                    xb = xbuf[:, (ts % 2) * 2048:(ts % 2) * 2048 + 2048]
                    v.reduce_sum(stat1[:, ts:ts + 1], xb, axis=AX.X)
                    if tau >= 1 and sub == 0:
                        v.wait_ge(S["mmdn"], 128 * tau)
                    v.scalar_tensor_tensor(
                        out=sqscr[:, :], in0=xb, scalar=1.0, in1=xb,
                        op0=ALU.mult, op1=ALU.mult,
                        accum_out=stat2[:, ts:ts + 1]).then_inc(S["lna"], 1)
                    v.wait_ge(S["lna"], ts + 1)
                    v.tensor_scalar_mul(meanv[:, ts:ts + 1], stat1[:, ts:ts + 1], 1.0 / 2048)
                    VV[0] += 1
                    v.tensor_scalar_mul(em2[:, ts:ts + 1], stat2[:, ts:ts + 1], 1.0 / 2048).then_inc(S["vv"], 1)
                    v.wait_ge(S["vv"], VV[0])
                    VV[0] += 1
                    v.tensor_mul(varb[:, ts:ts + 1], meanv[:, ts:ts + 1], meanv[:, ts:ts + 1]).then_inc(S["vv"], 1)
                    v.wait_ge(S["vv"], VV[0])
                    v.tensor_sub(varb[:, ts:ts + 1], em2[:, ts:ts + 1], varb[:, ts:ts + 1]).then_inc(S["var"], 1)
                    v.wait_ge(S["rstd"], ts + 1)
                    v.scalar_tensor_tensor(
                        out=nmr[:, ts:ts + 1], in0=meanv[:, ts:ts + 1], scalar=-1.0,
                        in1=rstd[:, ts:ts + 1], op0=ALU.mult, op1=ALU.mult).then_inc(S["nmr"], 1)
            for oc in range(4):
                for j in range(2):
                    kp = tau * 8 + oc * 2 + j
                    v.wait_ge(S["ek"], kp + 1)
                    if tau >= 1 and oc == 0 and j == 0:
                        v.wait_ge(S["aprep"], tau * NU)
                    if kp >= 1:
                        v.wait_ge(S["kfs"], kp)
                    VV[0] += 1
                    v.tensor_scalar_add(kpost[:, :],
                                        ekb[:, (kp % 2) * 512:(kp % 2) * 512 + 512], 1.0).then_inc(S["vv"], 1)
                    v.wait_ge(S["vv"], VV[0])
                    VV[0] += 1
                    v.reciprocal(kpost2[:, :], kpost[:, :]).then_inc(S["vv"], 1)
                    v.wait_ge(S["vv"], VV[0])
                    v.tensor_scalar_min(
                        kf[:, oc * 1024 + j * 512:oc * 1024 + j * 512 + 512],
                        kpost2[:, :], 0.95).then_inc(S["kfs"], 1)
            for voc in range(4):
                for j in range(2):
                    for b in range(4):
                        vg = voc * 2 + j
                        trn = tau * 160 + 128 + vg * 4 + b
                        vs = j * 4 + b
                        v.wait_ge(S["trp"], trn + 1)
                        if tau >= 1 and vg == 0 and b == 0:
                            v.wait_ge(S["omm"], tau * NU)
                            v.wait_ge(S["umm"], tau * NU)
                        v.tensor_copy(
                            vtok[:, vs * 512 + voc * 128:vs * 512 + voc * 128 + 128],
                            trb[trn % 3][:, 0:128]).then_inc(S["trev"], 1)
            emit_scan(0, tau)
            for u0 in range(4):
                emit_args(u0, tau)
            emit_scan(1, tau)
            emit_args(4, tau)
            for it in PLANS[tau]:
                kind = it[0]
                if kind == "attn":
                    emit_attn_v(it[1], tau)
                elif kind == "gu" and it[3] % 2 == 1:
                    part, icH, sub = it[1], it[2], it[3]
                    j = sub // 2
                    ic = part * NICP + icH
                    hc = tau * 88 + part * 22 + icH * 2 + j
                    v.wait_ge(S["mmgu"], tau * 176 + ic * 4 + sub + 1)
                    v.wait_ge(S["silu"], hc + 1)
                    if part == 0 and icH == 0 and j == 0:
                        v.wait_ge(S["trp"], tau * 160 + 128)
                        v.wait_ge(S["xn"], 8 * (tau + 1))
                        v.wait_ge(S["lna"], 8 * (tau + 1))
                        v.wait_ge(S["mmdn"], 128 * tau)
                    v.scalar_tensor_tensor(
                        out=hbuf[:, icH * 1024 + j * 512:icH * 1024 + j * 512 + 512],
                        in0=ps_mm[gid((tau,) + it) % 4][:, :],
                        scalar=ust[:, ic:ic + 1],
                        in1=sgb[:, (j % 2) * 512:(j % 2) * 512 + 512],
                        op0=ALU.mult, op1=ALU.mult).then_inc(S["hcmb"], 1)
                elif kind == "dn":
                    part, ib = it[1], it[2]
                    dmc, j = ib // 2, ib % 2
                    ev_i = tau * 128 + part * 32 + ib
                    evg = part * 32 + ib
                    if tau == 1 and part == 3 and ib < 8:
                        emit_fold(ib)
                    v.wait_ge(S["mmdn"], ev_i + 1)
                    dnstg_free_wait(v, tau, evg)
                    if part == 0:
                        v.tensor_scalar_mul(
                            dnstg[:, (ev_i % 2) * 512:(ev_i % 2) * 512 + 512],
                            ps_mm[gid((tau,) + it) % 4][:, :],
                            dst[:, dmc:dmc + 1]).then_inc(S["dnev"], 1)
                    else:
                        rid = tau * 96 + (part - 1) * 32 + ib
                        v.wait_ge(P("dnpld", rid), 16 * (rid // 2 + 1))
                        v.scalar_tensor_tensor(
                            out=dnstg[:, (ev_i % 2) * 512:(ev_i % 2) * 512 + 512],
                            in0=ps_mm[gid((tau,) + it) % 4][:, :],
                            scalar=dst[:, dmc:dmc + 1],
                            in1=dnpld[:, (rid % 2) * 512:(rid % 2) * 512 + 512],
                            op0=ALU.mult, op1=ALU.add).then_inc(S["dnev"], 1)
                elif kind == "tr":
                    i = it[1]
                    tq = tau * 32 + i
                    v.wait_ge(S["trq"], tq + 1)
                    if tq >= 1:
                        v.wait_ge(P("dnsp", tq - 1), 16 * ((tq - 1) // 2 + 1))
                    trg = gid((tau, "tr", i))
                    v.tensor_copy(trstg[:, 0:512],
                                  ps_mm[trg % 4][:, :]).then_inc(S["trqe"], 1)
                elif kind == "cc":
                    v.wait_ge(S["aring"], 2 * NU)
                    v.wait_ge(S["dtot"], 1)
                    pk = None
                    for gg in range(4):
                        v.tensor_copy(spack[:, gg * 129:gg * 129 + 1], dtot4[:, gg:gg + 1])
                        pk = v.tensor_copy(spack[:, gg * 129 + 1:(gg + 1) * 129],
                                           s_state[:, gg * 128:(gg + 1) * 128])
                    pk.then_inc(S["pack"], 1)

        v.wait_ge(S["ea"], 1)

        def emit_cevac(cq):
            v.wait_ge(S["fmm"], cq + 1)
            if cq >= 16:
                oc = cq // 4 - 4
                v.wait_ge(S[f"outs{oc % 4}"], 16 * (oc // 4 + 1))
            v.tensor_copy(cstg[:, (cq % 16) * 512:(cq % 16) * 512 + 512],
                          ps_mm[gid((1, "corr", cq)) % 4][:, :]
                          ).then_inc(S["cev"], 1)

        def emit_scorr(uq):
            v.tensor_scalar_mul(
                scorb[:, (uq % 4) * 128:(uq % 4) * 128 + 128],
                rfold[:, (uq % 4) * 129 + 1:(uq % 4) * 129 + 129],
                eab[:, uq:uq + 1]).then_inc(S["scorr"], 1)

        for uq in range(4):
            emit_scorr(uq)
        for uq in range(64):
            emit_cevac(uq)
            if uq + 4 < 64:
                emit_scorr(uq + 4)

    bg.__exit__(None, None, None)
    return nc


_NC_CACHE = {}


def _get_nc():
    if "nc" not in _NC_CACHE:
        _NC_CACHE["nc"] = build_nc()
    return _NC_CACHE["nc"]


def _prep_inputs(kw):
    f = np.float32
    x = np.asarray(kw["L0_attn_hidden_states"], f).reshape(16384, 2048)
    Wq = np.asarray(kw["L0_attn_Wq"], f)
    Wk = np.asarray(kw["L0_attn_Wk"], f)
    Wv = np.asarray(kw["L0_attn_Wv"], f)
    gw = np.asarray(kw["L0_ffn_gate_w"], f)
    uw = np.asarray(kw["L0_ffn_up_w"], f)
    dw = np.asarray(kw["L0_ffn_down_w"], f)

    def proj_slabs(W, noc):
        a = W.reshape(noc, 128, 16, 128).transpose(0, 3, 2, 1)
        return np.ascontiguousarray(a.reshape(noc, 128, 2048))

    wq_s = proj_slabs(Wq, 16)
    wk_s = proj_slabs(Wk, 4)
    wv_s = proj_slabs(Wv, 4)

    def ffn_slab(W):
        return W.reshape(44, 128, 16, 128).transpose(0, 3, 2, 1)

    gs_ = ffn_slab(gw)
    us_ = ffn_slab(uw)
    wgu_s = np.ascontiguousarray(
        np.stack([gs_, us_], axis=2).reshape(44, 128, 4096))
    wd_s = np.ascontiguousarray(dw.T.reshape(44, 128, 2048))

    def colmajor(vec, n):
        return np.ascontiguousarray(np.asarray(vec, f).reshape(n, 128).T)

    base = {
        "wq_s": wq_s, "wk_s": wk_s, "wv_s": wv_s, "wgu_s": wgu_s, "wd_s": wd_s,
        "bqt": colmajor(kw["L0_attn_bq"], 16),
        "bkt": colmajor(kw["L0_attn_bk"], 4),
        "bvt": colmajor(kw["L0_attn_bv"], 4),
        "gst": colmajor(kw["L0_ffn_gate_s"], 44),
        "ust": colmajor(kw["L0_ffn_up_s"], 44),
        "dst": colmajor(kw["L0_ffn_down_s"], 16),
        "identf": np.eye(128, dtype=f),
        "mask8": np.ascontiguousarray(
            (np.arange(128)[None, :] >= np.arange(128)[:, None])[:, None, :]
            .repeat(4, axis=1).reshape(128, 512).astype(np.int8)),
    }
    maps = []
    for core in range(NCORE):
        m = dict(base)
        m["x"] = np.ascontiguousarray(x[core * 2048:(core + 1) * 2048])
        cmv = np.zeros((128, 8), f)
        cmv[:, :core] = 1.0
        m["cmask"] = cmv
        cm1 = np.zeros((128, 8), f)
        cm1[:, core:] = 1.0
        m["cmask1"] = cm1
        maps.append(m)
    return maps


def kernel(**inputs):
    nc = _get_nc()
    in_maps = _prep_inputs(inputs)
    res = run_bass_kernel_spmd(nc, in_maps, core_ids=list(range(NCORE)))
    out = np.concatenate([res.results[i]["out"] for i in range(NCORE)], axis=0)
    return out.reshape(1, 16384, 2048).astype(np.float32)
